# version 11
# baseline (speedup 1.0000x reference)
import numpy as np
from contextlib import ExitStack

import concourse.bass as bass
import concourse.mybir as mybir
from concourse.bass_utils import run_bass_kernel_spmd

F32 = mybir.dt.float32
BF16 = mybir.dt.bfloat16
I32 = mybir.dt.int32
AF = mybir.ActivationFunctionType
ALU = mybir.AluOpType
AX = mybir.AxisListType

S, D = 4096, 2048
H, KV, NB, NV = 16, 4, 8, 8
NT = S // 128
TQ = 1024
OT = TQ // 128
E, I = 16, 1408
NE = E + 2
NCI = I // 128
HC = D // 128
PAD = 64
QW_ = PAD + S + 128
CEXP = 17.1
NKC = S // 128


def _final_barrier(nc, al):
    with nc.Block() as blk:
        @blk.sync
        def _(sy):
            sy.nop()


def build_program(debug=False, phases=99, hlim=H, shz=0, c3gate=True):
    nc = bass.Bass("TRN2", target_bir_lowering=False)

    xseq_h = nc.dram_tensor("xseq", [S, D], F32, kind="ExternalInput")
    wq_h = nc.dram_tensor("wq", [H * NB, D], F32, kind="ExternalInput")
    wk_h = nc.dram_tensor("wk", [KV * NB, D], F32, kind="ExternalInput")
    wv_h = nc.dram_tensor("wv", [KV * NV, D], F32, kind="ExternalInput")
    wo_h = nc.dram_tensor("wo", [D, H * NV], F32, kind="ExternalInput")
    gw_h = nc.dram_tensor("gw", [E, D], F32, kind="ExternalInput")
    ve0_h = nc.dram_tensor("ve0", [H * NV, 1], F32, kind="ExternalInput")
    ve1_h = nc.dram_tensor("ve1", [H * NV, 1], F32, kind="ExternalInput")
    tau_h = nc.dram_tensor("tau", [1, 1], F32, kind="ExternalInput")
    qidx_h = nc.dram_tensor("qidx", [1, TQ], F32, kind="ExternalInput")
    cq_h = nc.dram_tensor("cq", [1, 1], I32, kind="ExternalInput")
    wgu_h = nc.dram_tensor("wgu", [NE, D, 2816], F32, kind="ExternalInput")
    wdn_h = nc.dram_tensor("wdn", [NE, I, D], F32, kind="ExternalInput")
    y_h = nc.dram_tensor("y", [TQ, D], F32, kind="ExternalOutput")
    qdr_h = nc.dram_tensor("qdr", [128, QW_], BF16)
    adr_h = nc.dram_tensor("adr", [128, TQ], F32)

    xseq, wqa, wka, wva, woa, gwa = (t.ap() for t in (xseq_h, wq_h, wk_h, wv_h, wo_h, gw_h))
    ve0a, ve1a, taua, qidxa, cqa = (t.ap() for t in (ve0_h, ve1_h, tau_h, qidx_h, cq_h))
    wgua, wdna, ya = wgu_h.ap(), wdn_h.ap(), y_h.ap()

    dbg = {}
    if debug:
        for name, shp in [("d_kvfm", [64, QW_]), ("d_qfm", [128, QW_]),
                          ("d_attn", [128, TQ]), ("d_x1", [128, HC * TQ]),
                          ("d_comb", [16, TQ]), ("d_y", [128, OT * D]),
                          ("d_eT", [128, TQ]), ("d_stmp", [128, TQ]),
                          ("d_qw0", [128, TQ]), ("d_qw1", [128, TQ]),
                          ("d_kw", [128, QW_]), ("d_vs", [128, KV * NKC * 33])]:
            dbg[name] = nc.dram_tensor(name, shp, F32, kind="ExternalOutput").ap()

    ctx = ExitStack()
    with ctx:
        al = ctx.enter_context

        def sb(name, shp, dt, st=None):
            return (st or ctx).enter_context(nc.sbuf_tensor(name, shp, dt))

        def psum(name, shp, dt, st=None):
            return (st or ctx).enter_context(nc.psum_tensor(name, shp, dt))

        def sem(name):
            return al(nc.semaphore(name))

        id_f = sb("id_f", [128, 128], F32)
        id_b = sb("id_b", [128, 128], BF16)
        ones_row = sb("ones_row", [1, 128], F32)
        ones_col = sb("ones_col", [128, 1], F32)
        gwT = sb("gwT", [128, HC * 16], F32)
        ve0c = sb("ve0c", [128, 1], F32)
        vembd = sb("vembd", [128, 1], F32)
        tau_sb = sb("tau_sb", [1, 1], F32)
        tau_r = sb("tau_r", [1, 8], F32)
        tau_rc = sb("tau_rc", [128, 1], F32)
        inv16t = sb("inv16t", [128, 1], F32)
        negC = sb("negC", [128, 1], F32)

        jraw = sb("jraw", [128, NKC], F32)
        jtau = sb("jtau", [128, NKC], F32)
        cq_sb = sb("cq_sb", [1, 1], I32)
        x_fme = sb("x_fme", [128, HC * TQ], F32)
        wgu_sb = [sb(f"wgu_sb{i}", [128, HC * 256], BF16) for i in range(3)]

        att1 = al(ExitStack())
        wqT = sb("wqT", [128, HC * 128], BF16, att1)
        wkT = sb("wkT", [128, HC * 32], BF16, att1)
        wvT = sb("wvT", [128, HC * 32], BF16, att1)
        woT = sb("woT", [128, HC * 128], BF16, att1)
        q_fm = sb("q_fm", [128, QW_], BF16, att1)
        kv_fm = sb("kv_fm", [64, QW_], BF16, att1)
        vs_aug = sb("vs_aug", [128, KV * NKC * 33], BF16, att1)
        pen = sb("pen", [128, NKC * 128], BF16, att1)
        jrow = sb("jrow", [1, NKC * 128], BF16, att1)
        recip_q = sb("recip_q", [1, TQ], BF16, att1)

        prep = al(ExitStack())
        wq_tm = sb("wq_tm", [128, D], BF16, prep)
        wk_tm = sb("wk_tm", [32, D], BF16, prep)
        wv_tm = sb("wv_tm", [32, D], BF16, prep)
        wo_tm = sb("wo_tm", [128, HC * 128], BF16, prep)
        gw_tm = sb("gw_tm", [16, D], F32, prep)
        qrow = sb("qrow", [1, TQ], F32, prep)
        rrow = sb("rrow", [1, TQ], F32, prep)
        qidx_b = sb("qidx_b", [128, TQ], F32, prep)
        pT = [psum(f"pT{i}", [128, 128], BF16, prep) for i in range(4)]
        pTf = [psum(f"pTf{i}", [128, 16], F32, prep) for i in range(2)]
        pB = [psum(f"pB{i}", [128, 512], F32, prep) for i in range(2)]

        sA = sem("sA")
        with nc.Block() as blk:
            @blk.gpsimd
            def _(g):
                g.dma_start(wq_tm[:, :], wqa[:, :]).then_inc(sA, 16)
                g.dma_start(wk_tm[:, :], wka[:, :]).then_inc(sA, 16)
                g.dma_start(wv_tm[:, :], wva[:, :]).then_inc(sA, 16)
                g.dma_start(wo_tm[:, :].rearrange("p (oc v) -> p oc v", oc=16),
                            woa.rearrange("(oc p) v -> p oc v", p=128)).then_inc(sA, 16)
                g.memset(id_f[:, :], 1.0)
                g.affine_select(out=id_f[:, :], in_=id_f[:, :], compare_op=ALU.is_equal,
                                fill=0.0, base=0, pattern=[[-1, 128]], channel_multiplier=1)
                g.memset(id_b[:, :], 1.0)
                g.affine_select(out=id_b[:, :], in_=id_b[:, :], compare_op=ALU.is_equal,
                                fill=0.0, base=0, pattern=[[-1, 128]], channel_multiplier=1)
                g.iota(jraw[:, :], [[128, NKC]], channel_multiplier=1,
                       allow_small_or_imprecise_dtypes=True)
                g.iota(jrow[:1, :], [[1, NKC * 128]], channel_multiplier=0,
                       allow_small_or_imprecise_dtypes=True)
                g.memset(ones_row[:1, :], 1.0)
                g.memset(ones_col[:, :], 1.0)
                g.memset(q_fm[:, :], 0.0)
                g.memset(kv_fm[:, :], 0.0)
                g.wait_ge(sA, 160)

            @blk.sync
            def _(sy):
                sy.dma_start(gw_tm[:, :], gwa[:, :]).then_inc(sA, 16)
                sy.dma_start(ve0c[:, :], ve0a[:, :]).then_inc(sA, 16)
                sy.dma_start(vembd[:, :], ve1a[:, :]).then_inc(sA, 16)
                sy.dma_start(tau_sb[:1, :], taua[:1, :]).then_inc(sA, 16)
                sy.dma_start(qrow[:1, :], qidxa[:1, :]).then_inc(sA, 16)
                sy.dma_start(cq_sb[:1, :], cqa[:1, :]).then_inc(sA, 16)
                sy.wait_ge(sA, 160)

        sA2t = sem("sA2t")
        sA2v = sem("sA2v")
        sA2f = sem("sA2f")
        tp_jobs = []
        for hc in range(HC):
            tp_jobs.append((wq_tm[:, hc * 128:(hc + 1) * 128],
                            wqT[:, hc * 128:(hc + 1) * 128], id_b[:, :], False, 128))
        for hc in range(HC):
            tp_jobs.append((wk_tm[:, hc * 128:(hc + 1) * 128],
                            wkT[:, hc * 32:(hc + 1) * 32], id_b[:32, :32], False, 32))
        for hc in range(HC):
            tp_jobs.append((wv_tm[:, hc * 128:(hc + 1) * 128],
                            wvT[:, hc * 32:(hc + 1) * 32], id_b[:32, :32], False, 32))
        for hc in range(HC):
            tp_jobs.append((wo_tm[:, hc * 128:(hc + 1) * 128],
                            woT[:, hc * 128:(hc + 1) * 128], id_b[:, :], False, 128))
        for hc in range(HC):
            tp_jobs.append((gw_tm[:, hc * 128:(hc + 1) * 128],
                            gwT[:, hc * 16:(hc + 1) * 16], id_f[:16, :16], True, 16))

        with nc.Block() as blk:
            @blk.tensor
            def _(t):
                kb = kf = 0
                for (src, dst, ident, isf, w_) in tp_jobs:
                    if isf:
                        if kf >= 2:
                            t.wait_ge(sA2v, 64 + kf - 1)
                        t.transpose(pTf[kf % 2][:, :w_], src, ident).then_inc(sA2t, 1)
                        kf += 1
                    else:
                        if kb >= 4:
                            t.wait_ge(sA2v, kb - 3)
                        t.transpose(pT[kb % 4][:, :w_], src, ident).then_inc(sA2t, 1)
                        kb += 1

            @blk.vector
            def _(v):
                v.memset(tau_r[:1, :], 1.0).then_inc(sA2f, 1)
                v.wait_ge(sA2f, 1)
                v.tensor_copy(tau_r[:1, 0:1], tau_sb[:1, :]).then_inc(sA2f, 1)
                v.wait_ge(sA2f, 2)
                v.reciprocal(tau_r[:1, :], tau_r[:1, :]).then_inc(sA2f, 1)
                v.wait_ge(sA2f, 3)
                v.tensor_scalar(rrow[:1, :], qrow[:1, :], 1.0, None, ALU.add).then_inc(sA2f, 1)
                v.wait_ge(sA2f, 4)
                v.reciprocal(rrow[:1, :], rrow[:1, :]).then_inc(sA2f, 1)
                v.wait_ge(sA2f, 5)
                v.tensor_sub(vembd[:, :], vembd[:, :], ve0c[:, :])
                kb = kf = 0
                for j, (src, dst, ident, isf, w_) in enumerate(tp_jobs):
                    v.wait_ge(sA2t, j + 1)
                    if isf:
                        v.tensor_copy(dst, pTf[kf % 2][:, :w_]).then_inc(sA2v, 1)
                        kf += 1
                    else:
                        v.tensor_copy(dst, pT[kb % 4][:, :w_]).then_inc(sA2v, 1)
                        kb += 1

        sA3t = sem("sA3t")
        sA3v = sem("sA3v")
        with nc.Block() as blk:
            @blk.tensor
            def _(t):
                t.matmul(pB[0][:, 0:1], ones_row[:1, :], tau_r[:1, 0:1],
                         start=True, stop=True).then_inc(sA3t, 1)
                t.wait_ge(sA3v, 1)
                t.matmul(pB[0][:, :], ones_row[:1, :], qrow[:1, 0:512],
                         start=True, stop=True).then_inc(sA3t, 1)
                t.matmul(pB[1][:, :], ones_row[:1, :], qrow[:1, 512:1024],
                         start=True, stop=True).then_inc(sA3t, 1)
                t.wait_ge(sA3v, 2)
                t.matmul(pB[0][:, :], ones_row[:1, :], rrow[:1, 0:512],
                         start=True, stop=True).then_inc(sA3t, 1)
                t.wait_ge(sA3v, 3)
                t.matmul(pB[1][:, :], ones_row[:1, :], rrow[:1, 512:1024],
                         start=True, stop=True).then_inc(sA3t, 1)

            @blk.vector
            def _(v):
                v.wait_ge(sA3t, 1)
                v.tensor_copy(tau_rc[:, :], pB[0][:, 0:1]).then_inc(sA3v, 1)
                v.wait_ge(sA3t, 2)
                v.tensor_copy(qidx_b[:, 0:512], pB[0][:, :]).then_inc(sA3v, 1)
                v.wait_ge(sA3t, 3)
                v.tensor_copy(qidx_b[:, 512:1024], pB[1][:, :]).then_inc(sA3v, 1)
                v.wait_ge(sA3t, 5)
                v.tensor_scalar(inv16t[:, :], tau_rc[:, :], 1.0 / 16.0, None, ALU.mult)
                v.tensor_scalar(negC[:, :], tau_rc[:, :], -CEXP, None, ALU.mult)
                v.tensor_scalar(jtau[:, :], jraw[:, :], tau_rc[:, :], None, ALU.mult)
                v.tensor_scalar(jrow[:1, :], jrow[:1, :], tau_r[:1, 0:1], None, ALU.mult)
                v.tensor_copy(recip_q[:1, :], rrow[:1, :])
                v.tensor_copy(jtau[:, :], jtau[:, :])
                for kc in range(NKC):
                    t0 = kc // 4
                    v.tensor_scalar(pen[:, kc * 128:(kc + 1) * 128],
                                    qidx_b[:, t0 * 128:t0 * 128 + 128],
                                    jraw[:, kc:kc + 1], -1e9, ALU.is_le, ALU.mult)
                v.memset(vs_aug[:, :], 0.0)
                v.memset(vs_aug[:, 32::33], 1.0)

        prep.close()

        if phases < 2:
            _final_barrier(nc, al)
            return nc
        bsc = al(ExitStack())
        x_tm = [sb(f"x_tm{r}", [128, D], F32, bsc) for r in range(2)]
        x_f5 = sb("x_f5", [128, HC * 512], BF16, bsc)
        ptp = [psum(f"ptp{i}", [128, 512], F32, bsc) for i in range(2)]
        pq = psum("pq", [128, 512], F32, bsc)
        pk = psum("pk", [32, 512], F32, bsc)
        pv = psum("pv", [32, 512], F32, bsc)
        pvs = [psum(f"pvs{i}", [128, 256], BF16, bsc) for i in range(2)]

        sB_x = sem("sB_x")
        sB_xo = sem("sB_xo")
        sB_tt = sem("sB_tt")
        sB_tp = sem("sB_tp")
        sB_ev = sem("sB_ev")
        sB_mm = sem("sB_mm")
        sB_ac = sem("sB_ac")
        sB_vs = sem("sB_vs")
        vB_vse = sem("vB_vse")

        with nc.Block() as blk:
            @blk.sync
            def _(sy):
                for i in range(NT):
                    if i >= 2:
                        sy.wait_ge(sB_tp, 4 * (i - 1))
                    sy.dma_start(x_tm[i % 2][:, :],
                                 xseq[i * 128:(i + 1) * 128, :]).then_inc(sB_x, 16)

            @blk.gpsimd
            def _(g):
                with g.register("rcB") as rc, g.register("rcE") as rcE, g.register("rtB") as rt:
                    g.reg_load(rc, cq_sb[:1, :1])
                    g.reg_mul(rcE, rc, D)
                    for j in range(OT):
                        i = NT + j
                        g.wait_ge(sB_tp, 4 * (i - 1))
                        g.reg_add(rt, rcE, j * 4 * 128 * D)
                        g.dma_start(x_tm[i % 2][:, :],
                                    bass.AP(xseq_h, rt, [[D, 128], [1, D]])).then_inc(sB_xo, 16)

            @blk.tensor
            def _(t):
                for i in range(NT + OT):
                    if i < NT:
                        t.wait_ge(sB_x, 16 * (i + 1))
                    else:
                        t.wait_ge(sB_xo, 16 * (i - NT + 1))
                    for r in range(4):
                        k = 4 * i + r
                        if k >= 2:
                            t.wait_ge(sB_ev, k - 1)
                        ins = None
                        for cch in range(4):
                            ins = t.transpose(ptp[k % 2][:, cch * 128:(cch + 1) * 128],
                                              x_tm[i % 2][:, (4 * r + cch) * 128:(4 * r + cch + 1) * 128],
                                              id_f[:, :])
                        ins.then_inc(sB_tp, 1)
                    if i < NT and i % 4 == 3:
                        gidx = i // 4
                        t.wait_ge(sB_ev, 16 * (gidx + 1))
                        if gidx > 0:
                            t.wait_ge(sB_ac, 3 * gidx)
                        ins = None
                        for hcc in range(HC):
                            ins = t.matmul(pq[:, :], wqT[:, hcc * 128:(hcc + 1) * 128],
                                           x_f5[:, hcc * 512:(hcc + 1) * 512],
                                           start=(hcc == 0), stop=(hcc == HC - 1))
                        ins.then_inc(sB_mm, 1)
                        for hcc in range(HC):
                            ins = t.matmul(pk[:, :], wkT[:, hcc * 32:(hcc + 1) * 32],
                                           x_f5[:, hcc * 512:(hcc + 1) * 512],
                                           start=(hcc == 0), stop=(hcc == HC - 1))
                        ins.then_inc(sB_mm, 1)
                        for hcc in range(HC):
                            ins = t.matmul(pv[:, :], wvT[:, hcc * 32:(hcc + 1) * 32],
                                           x_f5[:, hcc * 512:(hcc + 1) * 512],
                                           start=(hcc == 0), stop=(hcc == HC - 1))
                        ins.then_inc(sB_mm, 1)
                t.wait_ge(sB_ac, 24)
                for grp in range(4):
                    if grp >= 2:
                        t.wait_ge(vB_vse, 4 * (grp - 1))
                    ins = None
                    for kcl in range(8):
                        kc = grp * 8 + kcl
                        ins = t.transpose(pvs[grp % 2][:, kcl * 32:(kcl + 1) * 32],
                                          kv_fm[32:64, PAD + kc * 128 + 1:PAD + kc * 128 + 129],
                                          id_b[32:64, 32:64])
                    ins.then_inc(sB_vs, 1)

            @blk.vector
            def _(v):
                for k in range(4 * NT):
                    v.wait_ge(sB_tp, k + 1)
                    gidx = k // 16
                    if gidx > 0:
                        v.wait_ge(sB_mm, 3 * gidx)
                    i, r = k // 4, k % 4
                    jloc = i % 4
                    dst = bass.AP(x_f5, (4 * r) * 512 + jloc * 128,
                                  [[HC * 512, 128], [512, 4], [1, 128]])
                    src = ptp[k % 2][:, :].rearrange("p (c r2) -> p c r2", c=4)
                    v.tensor_copy(dst, src).then_inc(sB_ev, 1)
                for j in range(OT):
                    for r in range(4):
                        k = 4 * (NT + j) + r
                        v.wait_ge(sB_tp, k + 1)
                        dst = bass.AP(x_fme, (4 * r) * TQ + j * 128,
                                      [[HC * TQ, 128], [TQ, 4], [1, 128]])
                        src = ptp[k % 2][:, :].rearrange("p (c r2) -> p c r2", c=4)
                        v.tensor_copy(dst, src).then_inc(sB_ev, 1)
                v.wait_ge(sB_ac, 24)
                v.tensor_scalar(q_fm[:, :], q_fm[:, :], inv16t[:, :], None, ALU.mult)
                for grp in range(4):
                    v.wait_ge(sB_vs, grp + 1)
                    for g4 in range(KV):
                        dstv = bass.AP(vs_aug, grp * 8 * 132 + g4 * 33,
                                       [[KV * NKC * 33, 128], [132, 8], [1, 8]])
                        srcv = bass.AP(pvs[grp % 2], g4 * 8,
                                       [[256, 128], [32, 8], [1, 8]])
                        v.tensor_copy(dstv, srcv).then_inc(vB_vse, 1)

            @blk.scalar
            def _(s):
                for gidx in range(8):
                    base = PAD + gidx * 512
                    s.wait_ge(sB_mm, 3 * gidx + 1)
                    s.activation(q_fm[:, base:base + 512], pq[:, :], AF.Tanh,
                                 bias=0.0, scale=tau_rc[:, :]).then_inc(sB_ac, 1)
                    s.wait_ge(sB_mm, 3 * gidx + 2)
                    s.activation(kv_fm[0:32, base:base + 512], pk[:, :], AF.Tanh,
                                 bias=0.0, scale=tau_rc[0:32, :]).then_inc(sB_ac, 1)
                    s.wait_ge(sB_mm, 3 * gidx + 3)
                    s.activation(kv_fm[32:64, base:base + 512], pv[:, :], AF.Sigmoid,
                                 bias=0.0, scale=tau_rc[0:32, :]).then_inc(sB_ac, 1)

        bsc.close()

        if debug:
            sDBG = sem("sDBG")
            with nc.Block() as blk:
                @blk.gpsimd
                def _(g):
                    g.dma_start(dbg["d_kvfm"][:, :], kv_fm[:, :]).then_inc(sDBG, 16)
                    g.dma_start(dbg["d_qfm"][:, :], q_fm[:, :]).then_inc(sDBG, 16)
                    g.wait_ge(sDBG, 32)

        if phases < 3:
            _final_barrier(nc, al)
            return nc
        att2b = al(ExitStack())
        attn_f = sb("attn_f", [128, TQ], F32, att2b)
        attn_b = sb("attn_b", [128, TQ], BF16, att2b)
        att2a = al(ExitStack())
        kw_sb = [sb(f"kw{s_}", [128, QW_], BF16, att2a) for s_ in range(2)]
        qsh = sb("qsh", [128, 3776], BF16, att2a)
        qw_sb = [[sb(f"qw{r}_{u}", [128, TQ], BF16, att2a) for u in range(2)] for r in range(2)]
        eT = [sb(f"eT{r}", [128, TQ], BF16, att2a) for r in range(3)]
        attn8 = sb("attn8", [8, TQ], F32, att2a)
        den8 = sb("den8", [8, TQ], F32, att2a)
        att2p = al(ExitStack())
        psT = [psum(f"psT{r}", [128, TQ], F32, att2p) for r in range(2)]
        po_lo = [psum(f"po_lo{r}", [72, 512], F32, att2p) for r in range(2)]
        po_hi = [psum(f"po_hi{r}", [72, 512], F32, att2p) for r in range(2)]
        den_row = den8[0:1, :]

        sC_kw = sem("sC_kw")
        sC_qs = sem("sC_qs")
        sC_qw = sem("sC_qw")
        tC_qk = sem("tC_qk")
        vC_st = sem("vC_st")
        cC_ex = sem("cC_ex")
        tC_pv = sem("tC_pv")
        tC_qh = sem("tC_qh")
        vC_dn = sem("vC_dn")
        cC_dn = sem("cC_dn")
        tC_dn = sem("tC_dn")
        vC_hp = sem("vC_hp")
        vC_f = sem("vC_f")
        sC_at = sem("sC_at")
        sD_gu = sem("sD_gu")
        tD_guf = sem("tD_guf")

        def kw_build(sy, g4):
            slot = g4 % 2
            for w_ in range(16):
                sy.dma_start(kw_sb[slot][w_::16, PAD:PAD + 4112],
                             kv_fm[g4 * 8:(g4 + 1) * 8,
                                   PAD - 31 + w_:PAD - 31 + w_ + 4112]).then_inc(sC_kw, 16)

        def seg_list(kc):
            t0 = kc // 4
            if kc < 16:
                return [(0, t0 * 128, 512), (1, 512, 1024)]
            return [(1, t0 * 128, 1024)]

        def pv_emit(t, n):
            hh2, kc2 = n // 32, n % 32
            t02 = kc2 // 4
            g42 = hh2 // 4
            t.wait_ge(cC_ex, n + 1)
            if kc2 == 0 and hh2 >= 2:
                t.wait_ge(vC_hp, hh2 - 1)
            po2 = hh2 % 2
            vsoff = (kc2 * KV + g42) * 33
            lhs = vs_aug[:, vsoff:vsoff + 33]
            ins = None
            if kc2 < 16:
                ins = t.matmul(po_lo[po2][0:33, t02 * 128:512], lhs,
                               eT[n % 3][:, t02 * 128:512],
                               start=(kc2 == 0), stop=(kc2 == 15))
                ins = t.matmul(po_hi[po2][0:33, 0:512], lhs, eT[n % 3][:, 512:1024],
                               start=(kc2 == 0), stop=(kc2 == 31))
            else:
                ins = t.matmul(po_hi[po2][0:33, t02 * 128 - 512:512], lhs,
                               eT[n % 3][:, t02 * 128:1024],
                               start=False, stop=(kc2 == 31))
            ins.then_inc(tC_pv, 1)

        with nc.Block() as blk:
            @blk.sync
            def _(sy):
                sy.dma_start(qdr_h.ap()[:, :], q_fm[:, :]).then_inc(sC_qs, 16)
                sy.wait_ge(sC_qs, 32)
                for hh in range(hlim):
                    if hh >= 2:
                        sy.wait_ge(tC_qk, 32 * (hh - 1))
                    for u in range(2):
                        for w_ in range(16):
                            off = hh * 8 * 3776 + (PAD - 31) + u * 16 + w_
                            sy.dma_start(
                                qw_sb[hh % 2][u][w_::16, :].rearrange(
                                    "p (t2 r2) -> p t2 r2", t2=8),
                                bass.AP(qsh, off, [[3776, 8], [512, 8], [1, 128]])
                            ).then_inc(sC_qw, 16)
                    if hh == 0:
                        kw_build(sy, 0)
                        if hlim > 4:
                            kw_build(sy, 1)
                    if hh == 5 and hlim > 8:
                        sy.wait_ge(tC_qk, 128)
                        kw_build(sy, 2)
                    if hh == 9 and hlim > 12:
                        sy.wait_ge(tC_qk, 256)
                        kw_build(sy, 3)

            @blk.gpsimd
            def _(g):
                with g.register("rqC") as rq:
                    g.reg_load(rq, cq_sb[:1, :1])
                    g.wait_ge(sC_qs, 16)
                    g.dma_start(qsh[:, :],
                                bass.AP(qdr_h, rq, [[QW_, 128], [1, 3776]])).then_inc(sC_qs, 16)
                for gc in range(3):
                    g.dma_start(wgu_sb[gc % 3][:, :].rearrange("p (hc j) -> p hc j", hc=HC),
                                wgua[0, :, gc * 256:(gc + 1) * 256].rearrange(
                                    "(hc p) j -> p hc j", p=128)).then_inc(sD_gu, 16)

            @blk.tensor
            def _(t):
                for hh in range(hlim):
                    g4 = hh // 4
                    t.wait_ge(sC_kw, 256 * (g4 + 1))
                    t.wait_ge(sC_qw, 512 * (hh + 1))
                    if (shz & 1) and hh >= 1:
                        t.wait_ge(sC_at, 16 * hh)
                    for kc in range(NKC):
                        n = 32 * hh + kc
                        rot = n % 2
                        if n >= 2:
                            t.wait_ge(cC_ex, n - 1)
                        ins = None
                        for (sgid, c0, c1) in seg_list(kc):
                            for u in range(2):
                                t.matmul(
                                    psT[rot][:, c0:c1],
                                    kw_sb[g4 % 2][:, PAD + 16 * u + kc * 128:
                                                  PAD + 16 * u + (kc + 1) * 128],
                                    qw_sb[hh % 2][u][:, c0:c1],
                                    start=(u == 0), stop=False)
                            ins = t.matmul(
                                psT[rot][:, c0:c1],
                                jrow[0:1, kc * 128:(kc + 1) * 128],
                                recip_q[0:1, c0:c1],
                                start=False, stop=True)
                        ins.then_inc(tC_qk, 1)
                        if kc >= 1:
                            pv_emit(t, n - 1)
                        if hh >= 1 and kc == 5:
                            t.wait_ge(cC_dn, hh)
                            t.matmul(po_lo[(hh - 1) % 2][64:72, :], ones_row[:1, 0:8],
                                     den_row[:1, 0:512], start=True, stop=True)
                            t.matmul(po_hi[(hh - 1) % 2][64:72, :], ones_row[:1, 0:8],
                                     den_row[:1, 512:1024], start=True,
                                     stop=True).then_inc(tC_dn, 1)
                    pv_emit(t, 32 * hh + 31)
                t.wait_ge(cC_dn, hlim)
                t.matmul(po_lo[(hlim - 1) % 2][64:72, :], ones_row[:1, 0:8],
                         den_row[:1, 0:512], start=True, stop=True)
                t.matmul(po_hi[(hlim - 1) % 2][64:72, :], ones_row[:1, 0:8],
                         den_row[:1, 512:1024], start=True, stop=True).then_inc(tC_dn, 1)

            @blk.vector
            def _(v):
                def v_epi1(hp):
                    v.wait_ge(tC_pv, 32 * (hp + 1))
                    v.tensor_copy(den_row[:1, 0:512], po_lo[hp % 2][32:33, :])
                    v.tensor_copy(den_row[:1, 512:1024],
                                  po_hi[hp % 2][32:33, :]).then_inc(vC_f, 1)
                    v.wait_ge(vC_f, 2 * hp + 1)
                    v.tensor_scalar(den_row[:1, :], den_row[:1, :], 1e-30, None,
                                    ALU.max).then_inc(vC_f, 1)
                    v.wait_ge(vC_f, 2 * hp + 2)
                    v.tensor_copy(den_row[:1, 0:1], den_row[:1, 0:1]).then_inc(vC_dn, 1)

                def v_epi2(hp):
                    v.wait_ge(tC_dn, hp + 1)
                    if hp >= 1:
                        v.wait_ge(sC_at, 16 * hp)
                    v.tensor_copy(den8[:, 0:512], po_lo[hp % 2][64:72, :])
                    v.tensor_copy(den8[:, 512:1024], po_hi[hp % 2][64:72, :])
                    v.tensor_mul(attn8[:, 0:512], po_lo[hp % 2][0:8, :], den8[:, 0:512])
                    v.tensor_mul(attn8[:, 512:1024], po_hi[hp % 2][0:8, :],
                                 den8[:, 512:1024]).then_inc(vC_hp, 1)

                for hh in range(hlim):
                    if (shz & 2) and hh >= 1:
                        v.wait_ge(sC_at, 16 * hh)
                    for kc in range(NKC):
                        n = 32 * hh + kc
                        t0 = kc // 4
                        if hh >= 1 and kc == 1:
                            v_epi1(hh - 1)
                        if hh >= 1 and kc == 7:
                            v_epi2(hh - 1)
                        v.wait_ge(tC_qk, n + 1)
                        v.tensor_add(psT[n % 2][:, t0 * 128:t0 * 128 + 128],
                                     psT[n % 2][:, t0 * 128:t0 * 128 + 128],
                                     pen[:, kc * 128:(kc + 1) * 128]).then_inc(vC_st, 1)
                v_epi1(hlim - 1)
                v_epi2(hlim - 1)

            @blk.scalar
            def _(s):
                def s_den(hp):
                    s.wait_ge(vC_dn, hp + 1)
                    s.activation(den_row[:1, :], den_row[:1, :], AF.Ln,
                                 bias=0.0, scale=1.0)
                    s.activation(den_row[:1, :], den_row[:1, :], AF.Exp,
                                 bias=0.0, scale=-1.0).then_inc(cC_dn, 1)

                for hh in range(hlim):
                    if (shz & 4) and hh >= 1:
                        s.wait_ge(sC_at, 16 * hh)
                    for kc in range(NKC):
                        n = 32 * hh + kc
                        t0 = kc // 4
                        s.wait_ge(vC_st, n + 1)
                        if n >= 3:
                            s.wait_ge(tC_pv, n - 2)
                        s.activation(eT[n % 3][:, t0 * 128:1024],
                                     psT[n % 2][:, t0 * 128:1024],
                                     AF.Exp, bias=negC[:, :], scale=1.0).then_inc(cC_ex, 1)
                        if hh >= 1 and kc == 3:
                            s_den(hh - 1)
                        if hh >= 1 and kc == 8:
                            s.wait_ge(vC_hp, hh)
                            s.dma_start(adr_h.ap()[(hh - 1) * 8:hh * 8, :],
                                        attn8[:, :]).then_inc(sC_at, 16)
                s_den(hlim - 1)
                s.wait_ge(vC_hp, hlim)
                s.dma_start(adr_h.ap()[(hlim - 1) * 8:hlim * 8, :],
                            attn8[:, :]).then_inc(sC_at, 16)
                s.wait_ge(sC_at, 16 * hlim)
                s.dma_start(attn_f[0:8 * hlim, :],
                            adr_h.ap()[0:8 * hlim, :]).then_inc(sC_at, 16)

        if debug:
            sDBG2 = sem("sDBG2")
            with nc.Block() as blk:
                @blk.sync
                def _(sy):
                    sy.wait_ge(sC_at, 16 * (hlim + 1))
                    sy.dma_start(dbg["d_attn"][:, :], attn_f[:, :]).then_inc(sDBG2, 16)
                    sy.wait_ge(sDBG2, 16)
                if hlim == 2 and "d_eT" in dbg:
                    @blk.gpsimd
                    def _(g):
                        g.wait_ge(sDBG2, 16)
                        g.dma_start(dbg["d_kw"][0:64, 0:3072], qsh[0:64, 0:3072]).then_inc(sDBG2, 16)
                        g.wait_ge(sDBG2, 32)
                if hlim == 1 and "d_eT" in dbg:
                    @blk.gpsimd
                    def _(g):
                        g.wait_ge(sDBG2, 16)
                        g.dma_start(dbg["d_eT"][:, :], eT[1][:, :]).then_inc(sDBG2, 16)
                        g.dma_start(dbg["d_stmp"][:, :], eT[0][:, :]).then_inc(sDBG2, 16)
                        g.dma_start(dbg["d_qw0"][:, :], qw_sb[0][0][:, :]).then_inc(sDBG2, 16)
                        g.dma_start(dbg["d_qw1"][:, :], qw_sb[0][1][:, :]).then_inc(sDBG2, 16)
                        g.dma_start(dbg["d_kw"][:, :], kw_sb[0][:, :]).then_inc(sDBG2, 16)
                        g.dma_start(dbg["d_vs"][:, :], vs_aug[:, :]).then_inc(sDBG2, 16)
                        g.wait_ge(sDBG2, 112)

        if phases < 4:
            _final_barrier(nc, al)
            return nc
        att2p.close()
        c2p = al(ExitStack())
        pwo = [psum(f"pwo{i}", [128, 512], F32, c2p) for i in range(2)]
        vC2 = sem("vC2")
        tC2 = sem("tC2")
        with nc.Block() as blk:
            @blk.vector
            def _(v):
                v.wait_ge(sC_at, 16 * (H + 1))
                v.tensor_scalar(attn_b[:, :], attn_f[:, :], vembd[:, :], ve0c[:, :],
                                ALU.mult, ALU.add).then_inc(vC2, 1)
                for k in range(32):
                    oc, hf = k // 2, k % 2
                    v.wait_ge(tC2, k + 1)
                    v.tensor_add(x_fme[:, oc * TQ + hf * 512:oc * TQ + hf * 512 + 512],
                                 x_fme[:, oc * TQ + hf * 512:oc * TQ + hf * 512 + 512],
                                 pwo[k % 2][:, :]).then_inc(vC2, 1)

            @blk.tensor
            def _(t):
                t.wait_ge(vC2, 1)
                for k in range(32):
                    oc, hf = k // 2, k % 2
                    if k >= 2:
                        t.wait_ge(vC2, k)
                    t.matmul(pwo[k % 2][:, :], woT[:, oc * 128:(oc + 1) * 128],
                             attn_b[:, hf * 512:hf * 512 + 512],
                             start=True, stop=True).then_inc(tC2, 1)

        c2p.close()
        att2a.close()
        att2b.close()
        att1.close()

        late = al(ExitStack())
        x_bf = sb("x_bf", [128, HC * TQ], BF16, late)
        crow = sb("crow", [1, TQ], F32, late)
        cb_sb = sb("cb_sb", [128, TQ], F32, late)
        sg_tmp = [sb(f"sg{r}", [128, 512], BF16, late) for r in range(3)]
        u2_tmp = [sb(f"u2{r}", [128, 512], BF16, late) for r in range(3)]
        comb = sb("comb", [16, TQ], F32, late)

        if phases < 5:
            _final_barrier(nc, al)
            return nc
        c3s = al(ExitStack())
        xsq = [sb(f"xsq{r}", [128, 512], F32, c3s) for r in range(2)]
        nrow = sb("nrow", [1, TQ], F32, c3s)
        vrow = sb("vrow", [1, TQ], F32, c3s)
        tmprow = sb("tmprow", [1, TQ], F32, c3s)
        negm_b = sb("negm_b", [128, TQ], F32, c3s)
        rstd_b = sb("rstd_b", [128, TQ], F32, c3s)
        e16 = sb("e16", [128, 16], F32, c3s)
        p16 = sb("p16", [128, 16], F32, c3s)
        m16 = sb("m16", [128, 1], F32, c3s)
        s16 = sb("s16", [128, 8], F32, c3s)
        sfull = sb("sfull", [128, 1], F32, c3s)
        msk16 = sb("msk16", [128, 16], F32, c3s)
        ctm = sb("ctm", [128, 16], F32, c3s)
        pst = psum("pst", [1, TQ], F32, c3s)
        psq = psum("psq", [1, TQ], F32, c3s)
        pbc = psum("pbc", [128, TQ], F32, c3s)
        pga = psum("pga", [128, 64], F32, c3s)
        pct = psum("pct", [16, 256], F32, c3s)
        cC3 = sem("cC3")
        tC3 = sem("tC3")
        vC3 = sem("vC3")
        tC3g = sem("tC3g")
        vC3g = sem("vC3g")
        tC3c = sem("tC3c")
        vC3f = sem("vC3f")

        with nc.Block() as blk:
            @blk.scalar
            def _(s):
                for k in range(32):
                    oc, hf = k // 2, k % 2
                    if k >= 2:
                        s.wait_ge(tC3, k - 1)
                    s.square(xsq[k % 2][:, :],
                             x_fme[:, oc * TQ + hf * 512:oc * TQ + hf * 512 + 512]).then_inc(cC3, 1)
                s.wait_ge(vC3, 1)
                s.activation(vrow[:1, :], vrow[:1, :], AF.Ln, bias=0.0, scale=1.0)
                s.activation(vrow[:1, :], vrow[:1, :], AF.Exp, bias=0.0,
                             scale=-0.5).then_inc(cC3, 1)
                for tt in range(OT if c3gate else 0):
                    s.wait_ge(vC3g, 4 * tt + 1)
                    s.activation(e16[:, :], pga[:, (tt % 2) * 32:(tt % 2) * 32 + 16],
                                 AF.Exp, bias=m16[:, :],
                                 scale=1.0).then_inc(cC3, 1)
                    s.wait_ge(vC3f, 11 * tt + 5)
                    s.activation(sfull[:, :], sfull[:, :], AF.Ln, bias=0.0, scale=1.0)
                    s.activation(sfull[:, :], sfull[:, :], AF.Exp, bias=0.0,
                                 scale=-1.0).then_inc(cC3, 1)
                    for j in range(2):
                        oc = 2 * tt + j
                        s.wait_ge(vC3, 4 + oc)
                        s.activation(x_bf[:, oc * TQ:(oc + 1) * TQ],
                                     x_fme[:, oc * TQ:(oc + 1) * TQ],
                                     AF.Identity, bias=0.0, scale=1.0)

            @blk.tensor
            def _(t):
                for hf in range(2):
                    for oc in range(HC):
                        t.matmul(pst[:1, hf * 512:hf * 512 + 512], ones_col[:, :],
                                 x_fme[:, oc * TQ + hf * 512:oc * TQ + hf * 512 + 512],
                                 start=(oc == 0), stop=(oc == HC - 1))
                for k in range(32):
                    oc, hf = k // 2, k % 2
                    t.wait_ge(cC3, k + 1)
                    t.matmul(psq[:1, hf * 512:hf * 512 + 512], ones_col[:, :],
                             xsq[k % 2][:, :],
                             start=(oc == 0), stop=(oc == HC - 1)).then_inc(tC3, 1)
                t.wait_ge(vC3, 1)
                for i2 in range(2):
                    t.matmul(pbc[:, i2 * 512:(i2 + 1) * 512], ones_row[:1, :],
                             nrow[:1, i2 * 512:(i2 + 1) * 512], start=True,
                             stop=True).then_inc(tC3, 1) if i2 == 1 else \
                    t.matmul(pbc[:, i2 * 512:(i2 + 1) * 512], ones_row[:1, :],
                             nrow[:1, i2 * 512:(i2 + 1) * 512], start=True, stop=True)
                t.wait_ge(vC3, 2)
                t.wait_ge(cC3, 33)
                for i2 in range(2):
                    t.matmul(pbc[:, i2 * 512:(i2 + 1) * 512], ones_row[:1, :],
                             vrow[:1, i2 * 512:(i2 + 1) * 512], start=True,
                             stop=True).then_inc(tC3, 1) if i2 == 1 else \
                    t.matmul(pbc[:, i2 * 512:(i2 + 1) * 512], ones_row[:1, :],
                             vrow[:1, i2 * 512:(i2 + 1) * 512], start=True, stop=True)
                t.wait_ge(vC3, 3 + HC)
                for tt in range(OT if c3gate else 0):
                    if tt >= 2:
                        t.wait_ge(vC3g, 4 * (tt - 1))
                    ins = None
                    for oc in range(HC):
                        ins = t.matmul(pga[:, (tt % 2) * 32:(tt % 2) * 32 + 16],
                                       x_fme[:, oc * TQ + tt * 128:oc * TQ + (tt + 1) * 128],
                                       gwT[:, oc * 16:(oc + 1) * 16],
                                       start=(oc == 0), stop=(oc == HC - 1))
                    ins.then_inc(tC3g, 1)
                    t.wait_ge(vC3g, 4 * tt + 3)
                    t.transpose(pct[:, (tt % 2) * 128:(tt % 2) * 128 + 128],
                                ctm[:, :], id_f[:, :]).then_inc(tC3c, 1)

            @blk.vector
            def _(v):
                fc = [0]

                def vf(ins):
                    fc[0] += 1
                    ins.then_inc(vC3f, 1)
                    v.wait_ge(vC3f, fc[0])

                v.wait_ge(tC3, 32)
                vf(v.tensor_scalar(nrow[:1, :], pst[:1, :], -1.0 / D, None, ALU.mult))
                vf(v.tensor_scalar(vrow[:1, :], psq[:1, :], 1.0 / D, 1e-5, ALU.mult, ALU.add))
                vf(v.tensor_mul(tmprow[:1, :], nrow[:1, :], nrow[:1, :]))
                v.tensor_sub(vrow[:1, :], vrow[:1, :], tmprow[:1, :]).then_inc(vC3, 1)
                v.wait_ge(tC3, 33)
                v.tensor_copy(negm_b[:, :], pbc[:, :]).then_inc(vC3, 1)
                v.wait_ge(tC3, 34)
                v.tensor_copy(rstd_b[:, :], pbc[:, :]).then_inc(vC3, 1)
                for oc in range(HC):
                    v.tensor_add(x_fme[:, oc * TQ:(oc + 1) * TQ],
                                 x_fme[:, oc * TQ:(oc + 1) * TQ], negm_b[:, :])
                    v.tensor_mul(x_fme[:, oc * TQ:(oc + 1) * TQ],
                                 x_fme[:, oc * TQ:(oc + 1) * TQ],
                                 rstd_b[:, :]).then_inc(vC3, 1)
                for tt in range(OT if c3gate else 0):
                    v.wait_ge(tC3g, tt + 1)
                    vf(v.reduce_max(m16[:, :], pga[:, (tt % 2) * 32:(tt % 2) * 32 + 16],
                                    axis=AX.X, negate=True))
                    v.tensor_copy(m16[:, :], m16[:, :]).then_inc(vC3g, 1)
                    v.wait_ge(cC3, 34 + 2 * tt)
                    vf(v.reduce_sum(sfull[:, :], e16[:, :], axis=AX.X))
                    v.wait_ge(cC3, 35 + 2 * tt)
                    vf(v.tensor_scalar(p16[:, :], e16[:, :], sfull[:, :], None, ALU.mult))
                    vf(v.reduce_max(m16[:, :], p16[:, :], axis=AX.X, negate=True))
                    vf(v.tensor_scalar(msk16[:, :], p16[:, :], m16[:, :], None, ALU.add))
                    vf(v.tensor_scalar(msk16[:, :], msk16[:, :], 0.0, 1.0, ALU.is_lt, ALU.mult))
                    vf(v.tensor_mul(msk16[:, :], p16[:, :], msk16[:, :]))
                    v.tensor_copy(msk16[:, :], msk16[:, :]).then_inc(vC3g, 1)
                    vf(v.reduce_max(m16[:, :], msk16[:, :], axis=AX.X, negate=True))
                    vf(v.tensor_scalar(msk16[:, :], p16[:, :], m16[:, :], None, ALU.add))
                    vf(v.tensor_scalar(msk16[:, :], msk16[:, :], 0.0, 1.0, ALU.is_ge, ALU.mult))
                    vf(v.tensor_mul(ctm[:, :], p16[:, :], msk16[:, :]))
                    v.tensor_copy(ctm[:, 0:1], ctm[:, 0:1]).then_inc(vC3g, 1)
                    v.wait_ge(tC3c, tt + 1)
                    v.tensor_copy(comb[:, tt * 128:(tt + 1) * 128],
                                  pct[:, (tt % 2) * 128:(tt % 2) * 128 + 128]).then_inc(vC3g, 1)

        c3s.close()

        if debug:
            sDBG3 = sem("sDBG3")
            with nc.Block() as blk:
                @blk.sync
                def _(sy):
                    sy.dma_start(dbg["d_x1"][:, :], x_fme[:, :]).then_inc(sDBG3, 16)
                    sy.dma_start(dbg["d_comb"][:, :], comb[:, :]).then_inc(sDBG3, 16)
                    sy.dma_start(dbg["d_qfm"][:, 0:512], rstd_b[:, 0:512]).then_inc(sDBG3, 16)
                    sy.wait_ge(sDBG3, 48)

        if phases < 6:
            _final_barrier(nc, al)
            return nc
        c4s = al(ExitStack())
        pcy0 = psum("pcy0", [128, 128], F32, c4s)
        pcyr = [psum(f"pcyr{i}", [128, 128], F32, c4s) for i in range(2)]
        tC4 = sem("tC4")
        vC4 = sem("vC4")

        def sigma(d):
            tt, oc = d // 16, d % 16
            return oc * 8 + tt

        cycles = []
        seen = [False] * 128
        for d0 in range(128):
            if seen[d0]:
                continue
            cyc = [d0]
            seen[d0] = True
            d = sigma(d0)
            while d != d0:
                cyc.append(d)
                seen[d] = True
                d = sigma(d)
            cycles.append(cyc)

        t_plan = []
        v_plan = []
        gt = gv = 0
        wk_global = 0
        last_ev_of_slot = {0: None, 1: None}
        for cyc in cycles:
            L = len(cyc)
            t_plan.append(("c0", cyc[0], gv))
            gt += 1
            for ii in range(1, L):
                slot = wk_global % 2
                w_ev = last_ev_of_slot[slot]
                t_plan.append((slot, cyc[ii], (w_ev + 1) if w_ev is not None else 0))
                gt += 1
                v_plan.append((cyc[ii - 1], slot, gt))
                last_ev_of_slot[slot] = gv
                gv += 1
                wk_global += 1
            v_plan.append((cyc[L - 1], "c0", gt))
            gv += 1

        with nc.Block() as blk:
            @blk.tensor
            def _(t):
                for (slot, srcb, w_ev) in t_plan:
                    if w_ev > 0:
                        t.wait_ge(vC4, w_ev)
                    dstp = pcy0 if slot == "c0" else pcyr[slot]
                    t.transpose(dstp[:, :], x_fme[:, srcb * 128:(srcb + 1) * 128],
                                id_f[:, :]).then_inc(tC4, 1)

            @blk.vector
            def _(v):
                for (dstb, slot, w_tp) in v_plan:
                    v.wait_ge(tC4, w_tp)
                    srcp = pcy0 if slot == "c0" else pcyr[slot]
                    v.tensor_copy(x_fme[:, dstb * 128:(dstb + 1) * 128],
                                  srcp[:, :]).then_inc(vC4, 1)

        c4s.close()

        if phases < 7:
            _final_barrier(nc, al)
            return nc
        dsc = al(ExitStack())
        wd_sb = sb("wd_sb", [128, NCI * D], BF16, dsc)
        h_sb = sb("h_sb", [128, NCI * TQ], BF16, dsc)

        pg_ = [psum(f"pg{i}", [128, 512], F32, dsc) for i in range(2)]
        pu_ = [psum(f"pu{i}", [128, 512], F32, dsc) for i in range(2)]
        pdall = psum("pdall", [128, 2048], F32, dsc)
        sD_wd = sem("sD_wd")
        sD_cr = sem("sD_cr")
        tD_g = sem("tD_g")
        tD_u = sem("tD_u")
        cD_sl = sem("cD_sl")
        vD_u2 = sem("vD_u2")
        gD_h = sem("gD_h")
        tD_cb = sem("tD_cb")
        vD_cb = sem("vD_cb")
        tD_dm = sem("tD_dm")
        vD_pd = sem("vD_pd")

        with nc.Block() as blk:
            @blk.gpsimd
            def _(g):
                for ee in range(NE):
                    if ee > 0:
                        g.wait_ge(tD_dm, 32 * ee)
                    for part in range(2):
                        ic0 = part * 6
                        icn = 6 if part == 0 else 5
                        g.dma_start(
                            wd_sb[:, ic0 * D:(ic0 + icn) * D].rearrange(
                                "p (ic o) -> p ic o", ic=icn),
                            wdna[ee, ic0 * 128:(ic0 + icn) * 128, :].rearrange(
                                "(ic p) o -> p ic o", p=128)).then_inc(sD_wd, 16)
                    for c in range(NCI):
                        gc = ee * NCI + c
                        if gc + 3 < NE * NCI:
                            ee2, c2 = (gc + 3) // NCI, (gc + 3) % NCI
                            g.wait_ge(tD_u, 2 * (gc + 1))
                            g.dma_start(
                                wgu_sb[gc % 3][:, :].rearrange("p (hc j) -> p hc j", hc=HC),
                                wgua[ee2, :, c2 * 256:(c2 + 1) * 256].rearrange(
                                    "(hc p) j -> p hc j", p=128)).then_inc(sD_gu, 16)


            @blk.scalar
            def _(s):
                s.dma_start(crow[:1, :], comb[0:1, :]).then_inc(sD_cr, 16)
                for ee in range(NE):
                    if ee + 1 < E:
                        s.wait_ge(tD_cb, ee + 1)
                        s.dma_start(crow[:1, :], comb[ee + 1:ee + 2, :]).then_inc(sD_cr, 16)
                    for c in range(NCI):
                        for hf in range(2):
                            idx = 2 * (ee * NCI + c) + hf
                            s.wait_ge(tD_g, idx + 1)
                            if idx >= 3:
                                s.wait_ge(gD_h, idx - 2)
                            s.activation(sg_tmp[idx % 3][:, :], pg_[idx % 2][:, :],
                                         AF.Silu, bias=0.0, scale=1.0).then_inc(cD_sl, 1)

            @blk.tensor
            def _(t):
                t.wait_ge(sD_cr, 16)
                t.matmul(pg_[0][:, 0:512], ones_row[:1, :],
                         crow[:1, 0:512], start=True, stop=True)
                t.matmul(pg_[1][:, 0:512], ones_row[:1, :],
                         crow[:1, 512:1024], start=True,
                         stop=True).then_inc(tD_cb, 1)
                for ee in range(NE):
                    for c in range(NCI):
                        gc = ee * NCI + c
                        t.wait_ge(sD_gu, 16 * (gc + 1))
                        if c == 0 and ee < E:
                            t.wait_ge(vD_cb, ee + 1)
                        if gc >= 1:
                            t.wait_ge(cD_sl, 2 * gc)
                        last = [None, None]
                        for hf in range(2):
                            for hcc in range(HC):
                                last[hf] = t.matmul(
                                    pg_[hf][:, :],
                                    wgu_sb[gc % 3][:, hcc * 256:hcc * 256 + 128],
                                    x_bf[:, hcc * TQ + hf * 512:hcc * TQ + hf * 512 + 512],
                                    start=(hcc == 0), stop=(hcc == HC - 1))
                            last[hf].then_inc(tD_g, 1)
                        if gc >= 1:
                            t.wait_ge(vD_u2, 2 * gc)
                        for hf in range(2):
                            for hcc in range(HC):
                                last[hf] = t.matmul(
                                    pu_[hf][:, :],
                                    wgu_sb[gc % 3][:, hcc * 256 + 128:hcc * 256 + 256],
                                    x_bf[:, hcc * TQ + hf * 512:hcc * TQ + hf * 512 + 512],
                                    start=(hcc == 0), stop=(hcc == HC - 1))
                            last[hf].then_inc(tD_u, 1)
                    if ee + 1 < E:
                        t.wait_ge(cD_sl, 22 * (ee + 1))
                        t.wait_ge(sD_cr, 16 * (ee + 2))
                        t.matmul(pg_[0][:, 0:512], ones_row[:1, :],
                                 crow[:1, 0:512], start=True, stop=True)
                        t.matmul(pg_[1][:, 0:512], ones_row[:1, :],
                                 crow[:1, 512:1024], start=True,
                                 stop=True).then_inc(tD_cb, 1)
                    t.wait_ge(gD_h, 22 * (ee + 1))
                    t.wait_ge(sD_wd, 32 * (ee + 1))
                    for tt in range(OT):
                        for og in range(4):
                            gidx = (8 * ee + tt) * 4 + og
                            if gidx >= 4:
                                t.wait_ge(vD_pd, gidx - 3)
                            ins = None
                            for ic in range(NCI):
                                lhs = h_sb[:, ic * TQ + tt * 128:ic * TQ + (tt + 1) * 128]
                                ins = t.matmul(
                                    pdall[:, og * 512:(og + 1) * 512], lhs,
                                    wd_sb[:, ic * D + og * 512:ic * D + og * 512 + 512],
                                    start=(ic == 0), stop=(ic == NCI - 1))
                            ins.then_inc(tD_dm, 1)

            @blk.vector
            def _(v):
                for ee in range(NE):
                    if ee < E:
                        v.wait_ge(tD_cb, ee + 1)
                        v.tensor_copy(cb_sb[:, 0:512], pg_[0][:, 0:512])
                        v.tensor_copy(cb_sb[:, 512:1024], pg_[1][:, 0:512]).then_inc(vD_cb, 1)
                    if ee == E:
                        v.memset(cb_sb[:, :], 1.0)
                    for c in range(NCI):
                        for hf in range(2):
                            idx = 2 * (ee * NCI + c) + hf
                            v.wait_ge(tD_u, idx + 1)
                            v.tensor_mul(u2_tmp[idx % 3][:, :], pu_[idx % 2][:, :],
                                         cb_sb[:, hf * 512:hf * 512 + 512]).then_inc(vD_u2, 1)
                            v.wait_ge(cD_sl, idx + 1)
                            if ee > 0:
                                v.wait_ge(tD_dm, 32 * ee)
                            v.tensor_mul(h_sb[:, c * TQ + hf * 512:c * TQ + hf * 512 + 512],
                                         sg_tmp[idx % 3][:, :],
                                         u2_tmp[idx % 3][:, :]).then_inc(gD_h, 1)
                    for tt in range(OT):
                        for og in range(4):
                            gidx = (8 * ee + tt) * 4 + og
                            v.wait_ge(tD_dm, gidx + 1)
                            v.tensor_add(
                                x_fme[:, tt * D + og * 512:tt * D + og * 512 + 512],
                                x_fme[:, tt * D + og * 512:tt * D + og * 512 + 512],
                                pdall[:, og * 512:(og + 1) * 512]).then_inc(vD_pd, 1)

        dsc.close()

        if phases < 8:
            _final_barrier(nc, al)
            return nc
        esc = al(ExitStack())
        ysq = [sb(f"ysq{r}", [128, D], F32, esc) for r in range(2)]
        srow1 = [sb(f"srow1{r}", [128, 1], F32, esc) for r in range(2)]
        srow2 = [sb(f"srow2{r}", [128, 1], F32, esc) for r in range(2)]
        srow3 = sb("srow3", [128, 1], F32, esc)
        nmr = [sb(f"nmr{r}", [128, 1], F32, esc) for r in range(2)]
        out_sb = [sb(f"out_sb{r}", [128, D], F32, esc) for r in range(2)]
        cSq = sem("cSq")
        vR = sem("vR")
        vV = sem("vV")
        cL = sem("cL")
        vN = sem("vN")
        cE = sem("cE")
        sE = sem("sE")
        vEf = sem("vEf")

        with nc.Block() as blk:
            @blk.scalar
            def _(s):
                s.square(ysq[0][:, :], x_fme[:, 0:D]).then_inc(cSq, 1)
                for tt in range(OT):
                    if tt + 1 < OT:
                        if tt >= 1:
                            s.wait_ge(vR, tt)
                        s.square(ysq[(tt + 1) % 2][:, :],
                                 x_fme[:, (tt + 1) * D:(tt + 2) * D]).then_inc(cSq, 1)
                    s.wait_ge(vV, tt + 1)
                    s.activation(srow2[tt % 2][:, :], srow2[tt % 2][:, :],
                                 AF.Ln, bias=0.0, scale=1.0)
                    s.activation(srow2[tt % 2][:, :], srow2[tt % 2][:, :],
                                 AF.Exp, bias=0.0, scale=-0.5).then_inc(cL, 1)
                    s.wait_ge(vN, tt + 1)
                    if tt >= 2:
                        s.wait_ge(sE, 16 * (tt - 1))
                    s.activation(out_sb[tt % 2][:, :], x_fme[:, tt * D:(tt + 1) * D],
                                 AF.Identity, bias=nmr[tt % 2][:, :],
                                 scale=srow2[tt % 2][:, :]).then_inc(cE, 1)

            @blk.vector
            def _(v):
                fe = [0]

                def ve_f(ins):
                    fe[0] += 1
                    ins.then_inc(vEf, 1)
                    v.wait_ge(vEf, fe[0])

                for tt in range(OT):
                    ve_f(v.reduce_sum(srow1[tt % 2][:, :],
                                      x_fme[:, tt * D:(tt + 1) * D], axis=AX.X))
                    v.wait_ge(cSq, tt + 1)
                    if tt >= 2:
                        v.wait_ge(cE, tt - 1)
                    v.reduce_sum(srow2[tt % 2][:, :], ysq[tt % 2][:, :],
                                 axis=AX.X).then_inc(vR, 1)
                    ve_f(v.tensor_scalar(srow1[tt % 2][:, :], srow1[tt % 2][:, :],
                                         -1.0 / D, None, ALU.mult))
                    ve_f(v.tensor_scalar(srow2[tt % 2][:, :], srow2[tt % 2][:, :],
                                         1.0 / D, 1e-5, ALU.mult, ALU.add))
                    ve_f(v.tensor_mul(srow3[:, :], srow1[tt % 2][:, :], srow1[tt % 2][:, :]))
                    ve_f(v.tensor_sub(srow2[tt % 2][:, :], srow2[tt % 2][:, :], srow3[:, :]))
                    v.tensor_copy(srow2[tt % 2][:, :], srow2[tt % 2][:, :]).then_inc(vV, 1)
                    v.wait_ge(cL, tt + 1)
                    v.tensor_mul(nmr[tt % 2][:, :], srow1[tt % 2][:, :],
                                 srow2[tt % 2][:, :]).then_inc(vN, 1)

            @blk.sync
            def _(sy):
                for tt in range(OT):
                    sy.wait_ge(cE, tt + 1)
                    sy.dma_start(ya[tt * 128:(tt + 1) * 128, :],
                                 out_sb[tt % 2][:, :]).then_inc(sE, 16)
                sy.wait_ge(sE, 16 * OT)

        if debug:
            sDBG4 = sem("sDBG4")
            with nc.Block() as blk:
                @blk.sync
                def _(sy):
                    sy.dma_start(dbg["d_y"][:, :], x_fme[:, :]).then_inc(sDBG4, 16)
                    sy.wait_ge(sDBG4, 16)

        esc.close()

    return nc



def _host_inputs(inputs):
    x = np.ascontiguousarray(np.asarray(inputs["L0_attn_hidden_states"], dtype=np.float32))
    Wq = np.ascontiguousarray(np.asarray(inputs["L0_attn_Wq"], dtype=np.float32))
    Wk = np.ascontiguousarray(np.asarray(inputs["L0_attn_Wk"], dtype=np.float32))
    Wv = np.ascontiguousarray(np.asarray(inputs["L0_attn_Wv"], dtype=np.float32))
    Wo = np.ascontiguousarray(np.asarray(inputs["L0_attn_Wo"], dtype=np.float32))
    ve0 = np.ascontiguousarray(np.asarray(inputs["L0_attn_v_emb0"], dtype=np.float32)).reshape(128, 1)
    ve1 = np.ascontiguousarray(np.asarray(inputs["L0_attn_v_emb1"], dtype=np.float32)).reshape(128, 1)
    tau = np.ascontiguousarray(np.asarray(inputs["L0_attn_tau"], dtype=np.float32)).reshape(1, 1)
    gwm = np.ascontiguousarray(np.asarray(inputs["L0_ffn_gate_w"], dtype=np.float32))
    wgu0 = np.asarray(inputs["L0_ffn_w_gate_up"], dtype=np.float32)
    wdn0 = np.asarray(inputs["L0_ffn_w_down"], dtype=np.float32)
    sgu = np.asarray(inputs["L0_ffn_shared_gate_up"], dtype=np.float32)
    sdn = np.asarray(inputs["L0_ffn_shared_down"], dtype=np.float32)

    wgu_all = np.empty((NE, D, 2816), dtype=np.float32)
    g_ = wgu0[:, :, :I].reshape(E, D, NCI, 128)
    u_ = wgu0[:, :, I:].reshape(E, D, NCI, 128)
    wgu_all[:E] = np.stack([g_, u_], axis=3).reshape(E, D, 2816)
    sg_ = sgu[:, :2816].reshape(D, 22, 128)
    su_ = sgu[:, 2816:].reshape(D, 22, 128)
    sh = np.stack([sg_, su_], axis=2).reshape(D, 22, 256)
    wgu_all[E] = sh[:, :NCI, :].reshape(D, 2816)
    wgu_all[E + 1] = sh[:, NCI:, :].reshape(D, 2816)
    wgu_all = np.ascontiguousarray(wgu_all)

    wdn_all = np.empty((NE, I, D), dtype=np.float32)
    wdn_all[:E] = wdn0
    wdn_all[E] = sdn[:I]
    wdn_all[E + 1] = sdn[I:]
    wdn_all = np.ascontiguousarray(wdn_all)

    in_maps = []
    for core in range(8):
        b, c = core // 4, core % 4
        qv = np.asarray([(4 * t + c) * 128 + r for t in range(OT) for r in range(128)],
                        dtype=np.float32).reshape(1, TQ)
        in_maps.append({
            "xseq": np.ascontiguousarray(x[b]),
            "wq": Wq, "wk": Wk, "wv": Wv, "wo": Wo, "gw": gwm,
            "ve0": ve0, "ve1": ve1, "tau": tau,
            "qidx": qv,
            "cq": np.asarray([[c * 128]], dtype=np.int32),
            "wgu": wgu_all, "wdn": wdn_all,
        })
    return in_maps


def kernel(**inputs):
    in_maps = _host_inputs(inputs)
    nc = build_program(debug=False)
    res = run_bass_kernel_spmd(nc, in_maps, core_ids=list(range(8)))
    out = np.empty((2, S, D), dtype=np.float32)
    for core in range(8):
        b, c = core // 4, core % 4
        yc = np.asarray(res.results[core]["y"], dtype=np.float32).reshape(OT, 128, D)
        for t in range(OT):
            a = 4 * t + c
            out[b, a * 128:(a + 1) * 128, :] = yc[t]
    return out


# version 12
# speedup vs baseline: 1.0429x; 1.0429x over previous
import numpy as np
from contextlib import ExitStack

import concourse.bass as bass
import concourse.mybir as mybir
from concourse.bass_utils import run_bass_kernel_spmd

F32 = mybir.dt.float32
BF16 = mybir.dt.bfloat16
I32 = mybir.dt.int32
AF = mybir.ActivationFunctionType
ALU = mybir.AluOpType
AX = mybir.AxisListType

S, D = 4096, 2048
H, KV, NB, NV = 16, 4, 8, 8
NT = S // 128
TQ = 1024
OT = TQ // 128
E, I = 16, 1408
NE = E + 2
NCI = I // 128
HC = D // 128
PAD = 64
QW_ = PAD + S + 128
CEXP = 17.1
NKC = S // 128


def _final_barrier(nc, al):
    with nc.Block() as blk:
        @blk.sync
        def _(sy):
            sy.nop()


def build_program(debug=False, phases=99, hlim=H, shz=0, c3gate=True):
    nc = bass.Bass("TRN2", target_bir_lowering=False)

    xseq_h = nc.dram_tensor("xseq", [S, D], F32, kind="ExternalInput")
    wq_h = nc.dram_tensor("wq", [H * NB, D], F32, kind="ExternalInput")
    wk_h = nc.dram_tensor("wk", [KV * NB, D], F32, kind="ExternalInput")
    wv_h = nc.dram_tensor("wv", [KV * NV, D], F32, kind="ExternalInput")
    wo_h = nc.dram_tensor("wo", [D, H * NV], F32, kind="ExternalInput")
    gw_h = nc.dram_tensor("gw", [E, D], F32, kind="ExternalInput")
    ve0_h = nc.dram_tensor("ve0", [H * NV, 1], F32, kind="ExternalInput")
    ve1_h = nc.dram_tensor("ve1", [H * NV, 1], F32, kind="ExternalInput")
    tau_h = nc.dram_tensor("tau", [1, 1], F32, kind="ExternalInput")
    qidx_h = nc.dram_tensor("qidx", [1, TQ], F32, kind="ExternalInput")
    cq_h = nc.dram_tensor("cq", [1, 1], I32, kind="ExternalInput")
    wgu_h = nc.dram_tensor("wgu", [NE, D, 2816], F32, kind="ExternalInput")
    wdn_h = nc.dram_tensor("wdn", [NE, I, D], F32, kind="ExternalInput")
    y_h = nc.dram_tensor("y", [TQ, D], F32, kind="ExternalOutput")
    qdr_h = nc.dram_tensor("qdr", [128, QW_], BF16)
    adr_h = nc.dram_tensor("adr", [128, TQ], F32)

    xseq, wqa, wka, wva, woa, gwa = (t.ap() for t in (xseq_h, wq_h, wk_h, wv_h, wo_h, gw_h))
    ve0a, ve1a, taua, qidxa, cqa = (t.ap() for t in (ve0_h, ve1_h, tau_h, qidx_h, cq_h))
    wgua, wdna, ya = wgu_h.ap(), wdn_h.ap(), y_h.ap()

    dbg = {}
    if debug:
        for name, shp in [("d_kvfm", [64, QW_]), ("d_qfm", [128, QW_]),
                          ("d_attn", [128, TQ]), ("d_x1", [128, HC * TQ]),
                          ("d_comb", [16, TQ]), ("d_y", [128, OT * D]),
                          ("d_eT", [128, TQ]), ("d_stmp", [128, TQ]),
                          ("d_qw0", [128, TQ]), ("d_qw1", [128, TQ]),
                          ("d_kw", [128, QW_]), ("d_vs", [128, KV * NKC * 33])]:
            dbg[name] = nc.dram_tensor(name, shp, F32, kind="ExternalOutput").ap()

    ctx = ExitStack()
    with ctx:
        al = ctx.enter_context

        def sb(name, shp, dt, st=None):
            return (st or ctx).enter_context(nc.sbuf_tensor(name, shp, dt))

        def psum(name, shp, dt, st=None):
            return (st or ctx).enter_context(nc.psum_tensor(name, shp, dt))

        def sem(name):
            return al(nc.semaphore(name))

        id_f = sb("id_f", [128, 128], F32)
        id_b = sb("id_b", [128, 128], BF16)
        ones_row = sb("ones_row", [1, 128], F32)
        ones_col = sb("ones_col", [128, 1], F32)
        gwT = sb("gwT", [128, HC * 16], F32)
        ve0c = sb("ve0c", [128, 1], F32)
        vembd = sb("vembd", [128, 1], F32)
        tau_sb = sb("tau_sb", [1, 1], F32)
        tau_r = sb("tau_r", [1, 8], F32)
        tau_rc = sb("tau_rc", [128, 1], F32)
        inv16t = sb("inv16t", [128, 1], F32)
        negC = sb("negC", [128, 1], F32)

        jraw = sb("jraw", [128, NKC], F32)
        jtau = sb("jtau", [128, NKC], F32)
        cq_sb = sb("cq_sb", [1, 1], I32)
        x_fme = sb("x_fme", [128, HC * TQ], F32)
        wgu_sb = [sb(f"wgu_sb{i}", [128, HC * 256], BF16) for i in range(3)]

        att1 = al(ExitStack())
        wqT = sb("wqT", [128, HC * 128], BF16, att1)
        wkT = sb("wkT", [128, HC * 32], BF16, att1)
        wvT = sb("wvT", [128, HC * 32], BF16, att1)
        woT = sb("woT", [128, HC * 128], BF16, att1)
        q_fm = sb("q_fm", [128, QW_], BF16, att1)
        kv_fm = sb("kv_fm", [64, QW_], BF16, att1)
        vs_aug = sb("vs_aug", [128, KV * NKC * 33], BF16, att1)
        pen = sb("pen", [128, NKC * 128], BF16, att1)
        jrow = sb("jrow", [1, NKC * 128], BF16, att1)
        recip_q = sb("recip_q", [1, TQ], BF16, att1)

        prep = al(ExitStack())
        wq_tm = sb("wq_tm", [128, D], BF16, prep)
        wk_tm = sb("wk_tm", [32, D], BF16, prep)
        wv_tm = sb("wv_tm", [32, D], BF16, prep)
        wo_tm = sb("wo_tm", [128, HC * 128], BF16, prep)
        gw_tm = sb("gw_tm", [16, D], F32, prep)
        qrow = sb("qrow", [1, TQ], F32, prep)
        rrow = sb("rrow", [1, TQ], F32, prep)
        qidx_b = sb("qidx_b", [128, TQ], F32, prep)
        pT = [psum(f"pT{i}", [128, 128], BF16, prep) for i in range(4)]
        pTf = [psum(f"pTf{i}", [128, 16], F32, prep) for i in range(2)]
        pB = [psum(f"pB{i}", [128, 512], F32, prep) for i in range(2)]

        sA = sem("sA")
        with nc.Block() as blk:
            @blk.gpsimd
            def _(g):
                g.dma_start(wq_tm[:, :], wqa[:, :]).then_inc(sA, 16)
                g.dma_start(wk_tm[:, :], wka[:, :]).then_inc(sA, 16)
                g.dma_start(wv_tm[:, :], wva[:, :]).then_inc(sA, 16)
                g.dma_start(wo_tm[:, :].rearrange("p (oc v) -> p oc v", oc=16),
                            woa.rearrange("(oc p) v -> p oc v", p=128)).then_inc(sA, 16)
                g.memset(id_f[:, :], 1.0)
                g.affine_select(out=id_f[:, :], in_=id_f[:, :], compare_op=ALU.is_equal,
                                fill=0.0, base=0, pattern=[[-1, 128]], channel_multiplier=1)
                g.memset(id_b[:, :], 1.0)
                g.affine_select(out=id_b[:, :], in_=id_b[:, :], compare_op=ALU.is_equal,
                                fill=0.0, base=0, pattern=[[-1, 128]], channel_multiplier=1)
                g.iota(jraw[:, :], [[128, NKC]], channel_multiplier=1,
                       allow_small_or_imprecise_dtypes=True)
                g.iota(jrow[:1, :], [[1, NKC * 128]], channel_multiplier=0,
                       allow_small_or_imprecise_dtypes=True)
                g.memset(ones_row[:1, :], 1.0)
                g.memset(ones_col[:, :], 1.0)
                g.memset(q_fm[:, :], 0.0)
                g.memset(kv_fm[:, :], 0.0)
                g.wait_ge(sA, 160)

            @blk.sync
            def _(sy):
                sy.dma_start(gw_tm[:, :], gwa[:, :]).then_inc(sA, 16)
                sy.dma_start(ve0c[:, :], ve0a[:, :]).then_inc(sA, 16)
                sy.dma_start(vembd[:, :], ve1a[:, :]).then_inc(sA, 16)
                sy.dma_start(tau_sb[:1, :], taua[:1, :]).then_inc(sA, 16)
                sy.dma_start(qrow[:1, :], qidxa[:1, :]).then_inc(sA, 16)
                sy.dma_start(cq_sb[:1, :], cqa[:1, :]).then_inc(sA, 16)
                sy.wait_ge(sA, 160)

        sA2t = sem("sA2t")
        sA2v = sem("sA2v")
        sA2f = sem("sA2f")
        tp_jobs = []
        for hc in range(HC):
            tp_jobs.append((wq_tm[:, hc * 128:(hc + 1) * 128],
                            wqT[:, hc * 128:(hc + 1) * 128], id_b[:, :], False, 128))
        for hc in range(HC):
            tp_jobs.append((wk_tm[:, hc * 128:(hc + 1) * 128],
                            wkT[:, hc * 32:(hc + 1) * 32], id_b[:32, :32], False, 32))
        for hc in range(HC):
            tp_jobs.append((wv_tm[:, hc * 128:(hc + 1) * 128],
                            wvT[:, hc * 32:(hc + 1) * 32], id_b[:32, :32], False, 32))
        for hc in range(HC):
            tp_jobs.append((wo_tm[:, hc * 128:(hc + 1) * 128],
                            woT[:, hc * 128:(hc + 1) * 128], id_b[:, :], False, 128))
        for hc in range(HC):
            tp_jobs.append((gw_tm[:, hc * 128:(hc + 1) * 128],
                            gwT[:, hc * 16:(hc + 1) * 16], id_f[:16, :16], True, 16))

        with nc.Block() as blk:
            @blk.tensor
            def _(t):
                kb = kf = 0
                for (src, dst, ident, isf, w_) in tp_jobs:
                    if isf:
                        if kf >= 2:
                            t.wait_ge(sA2v, 64 + kf - 1)
                        t.transpose(pTf[kf % 2][:, :w_], src, ident).then_inc(sA2t, 1)
                        kf += 1
                    else:
                        if kb >= 4:
                            t.wait_ge(sA2v, kb - 3)
                        t.transpose(pT[kb % 4][:, :w_], src, ident).then_inc(sA2t, 1)
                        kb += 1

            @blk.vector
            def _(v):
                v.memset(tau_r[:1, :], 1.0).then_inc(sA2f, 1)
                v.wait_ge(sA2f, 1)
                v.tensor_copy(tau_r[:1, 0:1], tau_sb[:1, :]).then_inc(sA2f, 1)
                v.wait_ge(sA2f, 2)
                v.reciprocal(tau_r[:1, :], tau_r[:1, :]).then_inc(sA2f, 1)
                v.wait_ge(sA2f, 3)
                v.tensor_scalar(rrow[:1, :], qrow[:1, :], 1.0, None, ALU.add).then_inc(sA2f, 1)
                v.wait_ge(sA2f, 4)
                v.reciprocal(rrow[:1, :], rrow[:1, :]).then_inc(sA2f, 1)
                v.wait_ge(sA2f, 5)
                v.tensor_sub(vembd[:, :], vembd[:, :], ve0c[:, :])
                kb = kf = 0
                for j, (src, dst, ident, isf, w_) in enumerate(tp_jobs):
                    v.wait_ge(sA2t, j + 1)
                    if isf:
                        v.tensor_copy(dst, pTf[kf % 2][:, :w_]).then_inc(sA2v, 1)
                        kf += 1
                    else:
                        v.tensor_copy(dst, pT[kb % 4][:, :w_]).then_inc(sA2v, 1)
                        kb += 1

        sA3t = sem("sA3t")
        sA3v = sem("sA3v")
        with nc.Block() as blk:
            @blk.tensor
            def _(t):
                t.matmul(pB[0][:, 0:1], ones_row[:1, :], tau_r[:1, 0:1],
                         start=True, stop=True).then_inc(sA3t, 1)
                t.wait_ge(sA3v, 1)
                t.matmul(pB[0][:, :], ones_row[:1, :], qrow[:1, 0:512],
                         start=True, stop=True).then_inc(sA3t, 1)
                t.matmul(pB[1][:, :], ones_row[:1, :], qrow[:1, 512:1024],
                         start=True, stop=True).then_inc(sA3t, 1)
                t.wait_ge(sA3v, 2)
                t.matmul(pB[0][:, :], ones_row[:1, :], rrow[:1, 0:512],
                         start=True, stop=True).then_inc(sA3t, 1)
                t.wait_ge(sA3v, 3)
                t.matmul(pB[1][:, :], ones_row[:1, :], rrow[:1, 512:1024],
                         start=True, stop=True).then_inc(sA3t, 1)

            @blk.vector
            def _(v):
                v.wait_ge(sA3t, 1)
                v.tensor_copy(tau_rc[:, :], pB[0][:, 0:1]).then_inc(sA3v, 1)
                v.wait_ge(sA3t, 2)
                v.tensor_copy(qidx_b[:, 0:512], pB[0][:, :]).then_inc(sA3v, 1)
                v.wait_ge(sA3t, 3)
                v.tensor_copy(qidx_b[:, 512:1024], pB[1][:, :]).then_inc(sA3v, 1)
                v.wait_ge(sA3t, 5)
                v.tensor_scalar(inv16t[:, :], tau_rc[:, :], 1.0 / 16.0, None, ALU.mult)
                v.tensor_scalar(negC[:, :], tau_rc[:, :], -CEXP, None, ALU.mult)
                v.tensor_scalar(jtau[:, :], jraw[:, :], tau_rc[:, :], None, ALU.mult)
                v.tensor_scalar(jrow[:1, :], jrow[:1, :], tau_r[:1, 0:1], None, ALU.mult)
                v.tensor_copy(recip_q[:1, :], rrow[:1, :])
                v.tensor_copy(jtau[:, :], jtau[:, :])
                for kc in range(NKC):
                    t0 = kc // 4
                    v.tensor_scalar(pen[:, kc * 128:(kc + 1) * 128],
                                    qidx_b[:, t0 * 128:t0 * 128 + 128],
                                    jraw[:, kc:kc + 1], -1e9, ALU.is_le, ALU.mult)
                v.memset(vs_aug[:, :], 0.0)
                v.memset(vs_aug[:, 32::33], 1.0)

        prep.close()

        if phases < 2:
            _final_barrier(nc, al)
            return nc
        bsc = al(ExitStack())
        x_tm = [sb(f"x_tm{r}", [128, D], F32, bsc) for r in range(2)]
        x_f5 = sb("x_f5", [128, HC * 512], BF16, bsc)
        ptp = [psum(f"ptp{i}", [128, 512], F32, bsc) for i in range(2)]
        pq = psum("pq", [128, 512], F32, bsc)
        pk = psum("pk", [32, 512], F32, bsc)
        pv = psum("pv", [32, 512], F32, bsc)
        pvs = [psum(f"pvs{i}", [128, 256], BF16, bsc) for i in range(2)]

        sB_x = sem("sB_x")
        sB_xo = sem("sB_xo")
        sB_tt = sem("sB_tt")
        sB_tp = sem("sB_tp")
        sB_ev = sem("sB_ev")
        sB_mm = sem("sB_mm")
        sB_ac = sem("sB_ac")
        sB_vs = sem("sB_vs")
        vB_vse = sem("vB_vse")

        with nc.Block() as blk:
            @blk.sync
            def _(sy):
                for i in range(NT):
                    if i >= 2:
                        sy.wait_ge(sB_tp, 4 * (i - 1))
                    sy.dma_start(x_tm[i % 2][:, :],
                                 xseq[i * 128:(i + 1) * 128, :]).then_inc(sB_x, 16)

            @blk.gpsimd
            def _(g):
                with g.register("rcB") as rc, g.register("rcE") as rcE, g.register("rtB") as rt:
                    g.reg_load(rc, cq_sb[:1, :1])
                    g.reg_mul(rcE, rc, D)
                    for j in range(OT):
                        i = NT + j
                        g.wait_ge(sB_tp, 4 * (i - 1))
                        g.reg_add(rt, rcE, j * 4 * 128 * D)
                        g.dma_start(x_tm[i % 2][:, :],
                                    bass.AP(xseq_h, rt, [[D, 128], [1, D]])).then_inc(sB_xo, 16)

            @blk.tensor
            def _(t):
                for i in range(NT + OT):
                    if i < NT:
                        t.wait_ge(sB_x, 16 * (i + 1))
                    else:
                        t.wait_ge(sB_xo, 16 * (i - NT + 1))
                    for r in range(4):
                        k = 4 * i + r
                        if k >= 2:
                            t.wait_ge(sB_ev, k - 1)
                        ins = None
                        for cch in range(4):
                            ins = t.transpose(ptp[k % 2][:, cch * 128:(cch + 1) * 128],
                                              x_tm[i % 2][:, (4 * r + cch) * 128:(4 * r + cch + 1) * 128],
                                              id_f[:, :])
                        ins.then_inc(sB_tp, 1)
                    if i < NT and i % 4 == 3:
                        gidx = i // 4
                        t.wait_ge(sB_ev, 16 * (gidx + 1))
                        if gidx > 0:
                            t.wait_ge(sB_ac, 3 * gidx)
                        ins = None
                        for hcc in range(HC):
                            ins = t.matmul(pq[:, :], wqT[:, hcc * 128:(hcc + 1) * 128],
                                           x_f5[:, hcc * 512:(hcc + 1) * 512],
                                           start=(hcc == 0), stop=(hcc == HC - 1))
                        ins.then_inc(sB_mm, 1)
                        for hcc in range(HC):
                            ins = t.matmul(pk[:, :], wkT[:, hcc * 32:(hcc + 1) * 32],
                                           x_f5[:, hcc * 512:(hcc + 1) * 512],
                                           start=(hcc == 0), stop=(hcc == HC - 1))
                        ins.then_inc(sB_mm, 1)
                        for hcc in range(HC):
                            ins = t.matmul(pv[:, :], wvT[:, hcc * 32:(hcc + 1) * 32],
                                           x_f5[:, hcc * 512:(hcc + 1) * 512],
                                           start=(hcc == 0), stop=(hcc == HC - 1))
                        ins.then_inc(sB_mm, 1)
                t.wait_ge(sB_ac, 24)
                for grp in range(4):
                    if grp >= 2:
                        t.wait_ge(vB_vse, 4 * (grp - 1))
                    ins = None
                    for kcl in range(8):
                        kc = grp * 8 + kcl
                        ins = t.transpose(pvs[grp % 2][:, kcl * 32:(kcl + 1) * 32],
                                          kv_fm[32:64, PAD + kc * 128 + 1:PAD + kc * 128 + 129],
                                          id_b[32:64, 32:64])
                    ins.then_inc(sB_vs, 1)

            @blk.vector
            def _(v):
                for k in range(4 * NT):
                    v.wait_ge(sB_tp, k + 1)
                    gidx = k // 16
                    if gidx > 0:
                        v.wait_ge(sB_mm, 3 * gidx)
                    i, r = k // 4, k % 4
                    jloc = i % 4
                    dst = bass.AP(x_f5, (4 * r) * 512 + jloc * 128,
                                  [[HC * 512, 128], [512, 4], [1, 128]])
                    src = ptp[k % 2][:, :].rearrange("p (c r2) -> p c r2", c=4)
                    v.tensor_copy(dst, src).then_inc(sB_ev, 1)
                for j in range(OT):
                    for r in range(4):
                        k = 4 * (NT + j) + r
                        v.wait_ge(sB_tp, k + 1)
                        dst = bass.AP(x_fme, (4 * r) * TQ + j * 128,
                                      [[HC * TQ, 128], [TQ, 4], [1, 128]])
                        src = ptp[k % 2][:, :].rearrange("p (c r2) -> p c r2", c=4)
                        v.tensor_copy(dst, src).then_inc(sB_ev, 1)
                v.wait_ge(sB_ac, 24)
                v.tensor_scalar(q_fm[:, :], q_fm[:, :], inv16t[:, :], None, ALU.mult)
                for grp in range(4):
                    v.wait_ge(sB_vs, grp + 1)
                    for g4 in range(KV):
                        dstv = bass.AP(vs_aug, grp * 8 * 132 + g4 * 33,
                                       [[KV * NKC * 33, 128], [132, 8], [1, 8]])
                        srcv = bass.AP(pvs[grp % 2], g4 * 8,
                                       [[256, 128], [32, 8], [1, 8]])
                        v.tensor_copy(dstv, srcv).then_inc(vB_vse, 1)

            @blk.scalar
            def _(s):
                for gidx in range(8):
                    base = PAD + gidx * 512
                    s.wait_ge(sB_mm, 3 * gidx + 1)
                    s.activation(q_fm[:, base:base + 512], pq[:, :], AF.Tanh,
                                 bias=0.0, scale=tau_rc[:, :]).then_inc(sB_ac, 1)
                    s.wait_ge(sB_mm, 3 * gidx + 2)
                    s.activation(kv_fm[0:32, base:base + 512], pk[:, :], AF.Tanh,
                                 bias=0.0, scale=tau_rc[0:32, :]).then_inc(sB_ac, 1)
                    s.wait_ge(sB_mm, 3 * gidx + 3)
                    s.activation(kv_fm[32:64, base:base + 512], pv[:, :], AF.Sigmoid,
                                 bias=0.0, scale=tau_rc[0:32, :]).then_inc(sB_ac, 1)

        bsc.close()

        if debug:
            sDBG = sem("sDBG")
            with nc.Block() as blk:
                @blk.gpsimd
                def _(g):
                    g.dma_start(dbg["d_kvfm"][:, :], kv_fm[:, :]).then_inc(sDBG, 16)
                    g.dma_start(dbg["d_qfm"][:, :], q_fm[:, :]).then_inc(sDBG, 16)
                    g.wait_ge(sDBG, 32)

        if phases < 3:
            _final_barrier(nc, al)
            return nc
        att2b = al(ExitStack())
        attn_f = sb("attn_f", [128, TQ], F32, att2b)
        attn_b = sb("attn_b", [128, TQ], BF16, att2b)
        att2a = al(ExitStack())
        kw_sb = [sb(f"kw{s_}", [128, QW_], BF16, att2a) for s_ in range(2)]
        qsh = sb("qsh", [128, 3776], BF16, att2a)
        qw_sb = [[sb(f"qw{r}_{u}", [128, TQ], BF16, att2a) for u in range(2)] for r in range(2)]
        eT = [sb(f"eT{r}", [128, TQ], BF16, att2a) for r in range(3)]
        attn8 = sb("attn8", [8, TQ], F32, att2a)
        den8 = sb("den8", [8, TQ], F32, att2a)
        att2p = al(ExitStack())
        psT = [psum(f"psT{r}", [128, TQ], F32, att2p) for r in range(2)]
        po_lo = [psum(f"po_lo{r}", [72, 512], F32, att2p) for r in range(2)]
        po_hi = [psum(f"po_hi{r}", [72, 512], F32, att2p) for r in range(2)]
        den_row = den8[0:1, :]

        sC_kw = sem("sC_kw")
        sC_qs = sem("sC_qs")
        sC_qw = sem("sC_qw")
        tC_qk = sem("tC_qk")
        vC_st = sem("vC_st")
        cC_ex = sem("cC_ex")
        tC_pv = sem("tC_pv")
        tC_qh = sem("tC_qh")
        vC_dn = sem("vC_dn")
        cC_dn = sem("cC_dn")
        tC_dn = sem("tC_dn")
        vC_hp = sem("vC_hp")
        vC_f = sem("vC_f")
        sC_at = sem("sC_at")
        sD_gu = sem("sD_gu")
        tD_guf = sem("tD_guf")

        def kw_build(sy, g4):
            slot = g4 % 2
            for w_ in range(16):
                sy.dma_start(kw_sb[slot][w_::16, PAD:PAD + 4112],
                             kv_fm[g4 * 8:(g4 + 1) * 8,
                                   PAD - 31 + w_:PAD - 31 + w_ + 4112]).then_inc(sC_kw, 16)

        def seg_list(kc):
            t0 = kc // 4
            if kc < 16:
                return [(0, t0 * 128, 512), (1, 512, 1024)]
            return [(1, t0 * 128, 1024)]

        def pv_emit(t, n):
            hh2, kc2 = n // 32, n % 32
            t02 = kc2 // 4
            g42 = hh2 // 4
            t.wait_ge(cC_ex, n + 1)
            if kc2 == 0 and hh2 >= 2:
                t.wait_ge(vC_hp, hh2 - 1)
            po2 = hh2 % 2
            vsoff = (kc2 * KV + g42) * 33
            lhs = vs_aug[:, vsoff:vsoff + 33]
            ins = None
            if kc2 < 16:
                ins = t.matmul(po_lo[po2][0:33, t02 * 128:512], lhs,
                               eT[n % 3][:, t02 * 128:512],
                               start=(kc2 == 0), stop=(kc2 == 15))
                ins = t.matmul(po_hi[po2][0:33, 0:512], lhs, eT[n % 3][:, 512:1024],
                               start=(kc2 == 0), stop=(kc2 == 31))
            else:
                ins = t.matmul(po_hi[po2][0:33, t02 * 128 - 512:512], lhs,
                               eT[n % 3][:, t02 * 128:1024],
                               start=False, stop=(kc2 == 31))
            ins.then_inc(tC_pv, 1)

        with nc.Block() as blk:
            @blk.sync
            def _(sy):
                sy.dma_start(qdr_h.ap()[:, :], q_fm[:, :]).then_inc(sC_qs, 16)
                sy.wait_ge(sC_qs, 32)
                for hh in range(hlim):
                    if hh >= 2:
                        sy.wait_ge(tC_qk, 32 * (hh - 1))
                    for u in range(2):
                        for w_ in range(16):
                            off = hh * 8 * 3776 + (PAD - 31) + u * 16 + w_
                            sy.dma_start(
                                qw_sb[hh % 2][u][w_::16, :].rearrange(
                                    "p (t2 r2) -> p t2 r2", t2=8),
                                bass.AP(qsh, off, [[3776, 8], [512, 8], [1, 128]])
                            ).then_inc(sC_qw, 16)
                    if hh == 0:
                        kw_build(sy, 0)
                        if hlim > 4:
                            kw_build(sy, 1)
                    if hh == 5 and hlim > 8:
                        sy.wait_ge(tC_qk, 128)
                        kw_build(sy, 2)
                    if hh == 9 and hlim > 12:
                        sy.wait_ge(tC_qk, 256)
                        kw_build(sy, 3)

            @blk.gpsimd
            def _(g):
                with g.register("rqC") as rq:
                    g.reg_load(rq, cq_sb[:1, :1])
                    g.wait_ge(sC_qs, 16)
                    g.dma_start(qsh[:, :],
                                bass.AP(qdr_h, rq, [[QW_, 128], [1, 3776]])).then_inc(sC_qs, 16)
                for gc in range(3):
                    g.dma_start(wgu_sb[gc % 3][:, :].rearrange("p (hc j) -> p hc j", hc=HC),
                                wgua[0, :, gc * 256:(gc + 1) * 256].rearrange(
                                    "(hc p) j -> p hc j", p=128)).then_inc(sD_gu, 16)

            @blk.tensor
            def _(t):
                for hh in range(hlim):
                    g4 = hh // 4
                    t.wait_ge(sC_kw, 256 * (g4 + 1))
                    t.wait_ge(sC_qw, 512 * (hh + 1))
                    if (shz & 1) and hh >= 1:
                        t.wait_ge(sC_at, 16 * hh)
                    for kc in range(NKC):
                        n = 32 * hh + kc
                        rot = n % 2
                        if n >= 2:
                            t.wait_ge(cC_ex, n - 1)
                        ins = None
                        for (sgid, c0, c1) in seg_list(kc):
                            for u in range(2):
                                t.matmul(
                                    psT[rot][:, c0:c1],
                                    kw_sb[g4 % 2][:, PAD + 16 * u + kc * 128:
                                                  PAD + 16 * u + (kc + 1) * 128],
                                    qw_sb[hh % 2][u][:, c0:c1],
                                    start=(u == 0), stop=False)
                            ins = t.matmul(
                                psT[rot][:, c0:c1],
                                jrow[0:1, kc * 128:(kc + 1) * 128],
                                recip_q[0:1, c0:c1],
                                start=False, stop=True)
                        ins.then_inc(tC_qk, 1)
                        if kc >= 1:
                            pv_emit(t, n - 1)
                        if hh >= 1 and kc == 5:
                            t.wait_ge(cC_dn, hh)
                            t.matmul(po_lo[(hh - 1) % 2][64:72, :], ones_row[:1, 0:8],
                                     den_row[:1, 0:512], start=True, stop=True)
                            t.matmul(po_hi[(hh - 1) % 2][64:72, :], ones_row[:1, 0:8],
                                     den_row[:1, 512:1024], start=True,
                                     stop=True).then_inc(tC_dn, 1)
                    pv_emit(t, 32 * hh + 31)
                t.wait_ge(cC_dn, hlim)
                t.matmul(po_lo[(hlim - 1) % 2][64:72, :], ones_row[:1, 0:8],
                         den_row[:1, 0:512], start=True, stop=True)
                t.matmul(po_hi[(hlim - 1) % 2][64:72, :], ones_row[:1, 0:8],
                         den_row[:1, 512:1024], start=True, stop=True).then_inc(tC_dn, 1)

            @blk.vector
            def _(v):
                def v_epi1(hp):
                    v.wait_ge(tC_pv, 32 * (hp + 1))
                    v.tensor_copy(den_row[:1, 0:512], po_lo[hp % 2][32:33, :])
                    v.tensor_copy(den_row[:1, 512:1024],
                                  po_hi[hp % 2][32:33, :]).then_inc(vC_f, 1)
                    v.wait_ge(vC_f, 2 * hp + 1)
                    v.tensor_scalar(den_row[:1, :], den_row[:1, :], 1e-30, None,
                                    ALU.max).then_inc(vC_f, 1)
                    v.wait_ge(vC_f, 2 * hp + 2)
                    v.tensor_copy(den_row[:1, 0:1], den_row[:1, 0:1]).then_inc(vC_dn, 1)

                def v_epi2(hp):
                    v.wait_ge(tC_dn, hp + 1)
                    if hp >= 1:
                        v.wait_ge(sC_at, 16 * hp)
                    v.tensor_copy(den8[:, 0:512], po_lo[hp % 2][64:72, :])
                    v.tensor_copy(den8[:, 512:1024], po_hi[hp % 2][64:72, :])
                    v.tensor_mul(attn8[:, 0:512], po_lo[hp % 2][0:8, :], den8[:, 0:512])
                    v.tensor_mul(attn8[:, 512:1024], po_hi[hp % 2][0:8, :],
                                 den8[:, 512:1024]).then_inc(vC_hp, 1)

                for hh in range(hlim):
                    if (shz & 2) and hh >= 1:
                        v.wait_ge(sC_at, 16 * hh)
                    for kc in range(NKC):
                        n = 32 * hh + kc
                        t0 = kc // 4
                        if hh >= 1 and kc == 1:
                            v_epi1(hh - 1)
                        if hh >= 1 and kc == 7:
                            v_epi2(hh - 1)
                        v.wait_ge(tC_qk, n + 1)
                        v.tensor_add(psT[n % 2][:, t0 * 128:t0 * 128 + 128],
                                     psT[n % 2][:, t0 * 128:t0 * 128 + 128],
                                     pen[:, kc * 128:(kc + 1) * 128]).then_inc(vC_st, 1)
                v_epi1(hlim - 1)
                v_epi2(hlim - 1)

            @blk.scalar
            def _(s):
                def s_den(hp):
                    s.wait_ge(vC_dn, hp + 1)
                    s.activation(den_row[:1, :], den_row[:1, :], AF.Ln,
                                 bias=0.0, scale=1.0)
                    s.activation(den_row[:1, :], den_row[:1, :], AF.Exp,
                                 bias=0.0, scale=-1.0).then_inc(cC_dn, 1)

                for hh in range(hlim):
                    if (shz & 4) and hh >= 1:
                        s.wait_ge(sC_at, 16 * hh)
                    for kc in range(NKC):
                        n = 32 * hh + kc
                        t0 = kc // 4
                        s.wait_ge(vC_st, n + 1)
                        if n >= 3:
                            s.wait_ge(tC_pv, n - 2)
                        s.activation(eT[n % 3][:, t0 * 128:1024],
                                     psT[n % 2][:, t0 * 128:1024],
                                     AF.Exp, bias=negC[:, :], scale=1.0).then_inc(cC_ex, 1)
                        if hh >= 1 and kc == 3:
                            s_den(hh - 1)
                        if hh >= 1 and kc == 8:
                            s.wait_ge(vC_hp, hh)
                            s.dma_start(adr_h.ap()[(hh - 1) * 8:hh * 8, :],
                                        attn8[:, :]).then_inc(sC_at, 16)
                s_den(hlim - 1)
                s.wait_ge(vC_hp, hlim)
                s.dma_start(adr_h.ap()[(hlim - 1) * 8:hlim * 8, :],
                            attn8[:, :]).then_inc(sC_at, 16)
                s.wait_ge(sC_at, 16 * hlim)
                s.dma_start(attn_f[0:8 * hlim, :],
                            adr_h.ap()[0:8 * hlim, :]).then_inc(sC_at, 16)

        if debug:
            sDBG2 = sem("sDBG2")
            with nc.Block() as blk:
                @blk.sync
                def _(sy):
                    sy.wait_ge(sC_at, 16 * (hlim + 1))
                    sy.dma_start(dbg["d_attn"][:, :], attn_f[:, :]).then_inc(sDBG2, 16)
                    sy.wait_ge(sDBG2, 16)
                if hlim == 2 and "d_eT" in dbg:
                    @blk.gpsimd
                    def _(g):
                        g.wait_ge(sDBG2, 16)
                        g.dma_start(dbg["d_kw"][0:64, 0:3072], qsh[0:64, 0:3072]).then_inc(sDBG2, 16)
                        g.wait_ge(sDBG2, 32)
                if hlim == 1 and "d_eT" in dbg:
                    @blk.gpsimd
                    def _(g):
                        g.wait_ge(sDBG2, 16)
                        g.dma_start(dbg["d_eT"][:, :], eT[1][:, :]).then_inc(sDBG2, 16)
                        g.dma_start(dbg["d_stmp"][:, :], eT[0][:, :]).then_inc(sDBG2, 16)
                        g.dma_start(dbg["d_qw0"][:, :], qw_sb[0][0][:, :]).then_inc(sDBG2, 16)
                        g.dma_start(dbg["d_qw1"][:, :], qw_sb[0][1][:, :]).then_inc(sDBG2, 16)
                        g.dma_start(dbg["d_kw"][:, :], kw_sb[0][:, :]).then_inc(sDBG2, 16)
                        g.dma_start(dbg["d_vs"][:, :], vs_aug[:, :]).then_inc(sDBG2, 16)
                        g.wait_ge(sDBG2, 112)

        if phases < 4:
            _final_barrier(nc, al)
            return nc
        att2p.close()
        c2p = al(ExitStack())
        pwo = [psum(f"pwo{i}", [128, 512], F32, c2p) for i in range(2)]
        vC2 = sem("vC2")
        tC2 = sem("tC2")
        with nc.Block() as blk:
            @blk.vector
            def _(v):
                v.wait_ge(sC_at, 16 * (H + 1))
                v.tensor_scalar(attn_b[:, :], attn_f[:, :], vembd[:, :], ve0c[:, :],
                                ALU.mult, ALU.add).then_inc(vC2, 1)
                for k in range(32):
                    oc, hf = k // 2, k % 2
                    v.wait_ge(tC2, k + 1)
                    v.tensor_add(x_fme[:, oc * TQ + hf * 512:oc * TQ + hf * 512 + 512],
                                 x_fme[:, oc * TQ + hf * 512:oc * TQ + hf * 512 + 512],
                                 pwo[k % 2][:, :]).then_inc(vC2, 1)

            @blk.tensor
            def _(t):
                t.wait_ge(vC2, 1)
                for k in range(32):
                    oc, hf = k // 2, k % 2
                    if k >= 2:
                        t.wait_ge(vC2, k)
                    t.matmul(pwo[k % 2][:, :], woT[:, oc * 128:(oc + 1) * 128],
                             attn_b[:, hf * 512:hf * 512 + 512],
                             start=True, stop=True).then_inc(tC2, 1)

        c2p.close()
        att2a.close()
        att2b.close()
        att1.close()

        late = al(ExitStack())
        x_bf = sb("x_bf", [128, HC * TQ], BF16, late)
        crow = sb("crow", [1, TQ], F32, late)
        cb_sb = sb("cb_sb", [128, TQ], F32, late)
        sg_tmp = [sb(f"sg{r}", [128, 512], BF16, late) for r in range(3)]
        u2_tmp = [sb(f"u2{r}", [128, 512], BF16, late) for r in range(3)]
        comb = sb("comb", [16, TQ], F32, late)

        if phases < 5:
            _final_barrier(nc, al)
            return nc
        c3s = al(ExitStack())
        xsq = [sb(f"xsq{r}", [128, 512], F32, c3s) for r in range(2)]
        nrow = sb("nrow", [1, TQ], F32, c3s)
        vrow = sb("vrow", [1, TQ], F32, c3s)
        tmprow = sb("tmprow", [1, TQ], F32, c3s)
        negm_b = sb("negm_b", [128, TQ], F32, c3s)
        rstd_b = sb("rstd_b", [128, TQ], F32, c3s)
        e16 = sb("e16", [128, 16], F32, c3s)
        p16 = sb("p16", [128, 16], F32, c3s)
        m16 = sb("m16", [128, 1], F32, c3s)
        s16 = sb("s16", [128, 8], F32, c3s)
        sfull = sb("sfull", [128, 1], F32, c3s)
        msk16 = sb("msk16", [128, 16], F32, c3s)
        ctm = sb("ctm", [128, 16], F32, c3s)
        pst = psum("pst", [1, TQ], F32, c3s)
        psq = psum("psq", [1, TQ], F32, c3s)
        pbc = psum("pbc", [128, TQ], F32, c3s)
        pga = psum("pga", [128, 64], F32, c3s)
        pct = psum("pct", [16, 256], F32, c3s)
        cC3 = sem("cC3")
        tC3 = sem("tC3")
        vC3 = sem("vC3")
        tC3g = sem("tC3g")
        vC3g = sem("vC3g")
        tC3c = sem("tC3c")
        vC3f = sem("vC3f")

        with nc.Block() as blk:
            @blk.scalar
            def _(s):
                for k in range(32):
                    oc, hf = k // 2, k % 2
                    if k >= 2:
                        s.wait_ge(tC3, k - 1)
                    s.square(xsq[k % 2][:, :],
                             x_fme[:, oc * TQ + hf * 512:oc * TQ + hf * 512 + 512]).then_inc(cC3, 1)
                s.wait_ge(vC3, 1)
                s.activation(vrow[:1, :], vrow[:1, :], AF.Ln, bias=0.0, scale=1.0)
                s.activation(vrow[:1, :], vrow[:1, :], AF.Exp, bias=0.0,
                             scale=-0.5).then_inc(cC3, 1)
                for tt in range(OT if c3gate else 0):
                    s.wait_ge(vC3g, 4 * tt + 1)
                    s.activation(e16[:, :], pga[:, (tt % 2) * 32:(tt % 2) * 32 + 16],
                                 AF.Exp, bias=m16[:, :],
                                 scale=1.0).then_inc(cC3, 1)
                    s.wait_ge(vC3f, 11 * tt + 5)
                    s.activation(sfull[:, :], sfull[:, :], AF.Ln, bias=0.0, scale=1.0)
                    s.activation(sfull[:, :], sfull[:, :], AF.Exp, bias=0.0,
                                 scale=-1.0).then_inc(cC3, 1)

            @blk.tensor
            def _(t):
                for hf in range(2):
                    for oc in range(HC):
                        t.matmul(pst[:1, hf * 512:hf * 512 + 512], ones_col[:, :],
                                 x_fme[:, oc * TQ + hf * 512:oc * TQ + hf * 512 + 512],
                                 start=(oc == 0), stop=(oc == HC - 1))
                for k in range(32):
                    oc, hf = k // 2, k % 2
                    t.wait_ge(cC3, k + 1)
                    t.matmul(psq[:1, hf * 512:hf * 512 + 512], ones_col[:, :],
                             xsq[k % 2][:, :],
                             start=(oc == 0), stop=(oc == HC - 1)).then_inc(tC3, 1)
                t.wait_ge(vC3, 1)
                for i2 in range(2):
                    t.matmul(pbc[:, i2 * 512:(i2 + 1) * 512], ones_row[:1, :],
                             nrow[:1, i2 * 512:(i2 + 1) * 512], start=True,
                             stop=True).then_inc(tC3, 1) if i2 == 1 else \
                    t.matmul(pbc[:, i2 * 512:(i2 + 1) * 512], ones_row[:1, :],
                             nrow[:1, i2 * 512:(i2 + 1) * 512], start=True, stop=True)
                t.wait_ge(vC3, 2)
                t.wait_ge(cC3, 33)
                for i2 in range(2):
                    t.matmul(pbc[:, i2 * 512:(i2 + 1) * 512], ones_row[:1, :],
                             vrow[:1, i2 * 512:(i2 + 1) * 512], start=True,
                             stop=True).then_inc(tC3, 1) if i2 == 1 else \
                    t.matmul(pbc[:, i2 * 512:(i2 + 1) * 512], ones_row[:1, :],
                             vrow[:1, i2 * 512:(i2 + 1) * 512], start=True, stop=True)
                t.wait_ge(vC3, 3 + HC)
                for tt in range(OT if c3gate else 0):
                    if tt >= 2:
                        t.wait_ge(vC3g, 4 * (tt - 1))
                    ins = None
                    for oc in range(HC):
                        ins = t.matmul(pga[:, (tt % 2) * 32:(tt % 2) * 32 + 16],
                                       x_fme[:, oc * TQ + tt * 128:oc * TQ + (tt + 1) * 128],
                                       gwT[:, oc * 16:(oc + 1) * 16],
                                       start=(oc == 0), stop=(oc == HC - 1))
                    ins.then_inc(tC3g, 1)
                    t.wait_ge(vC3g, 4 * tt + 3)
                    t.transpose(pct[:, (tt % 2) * 128:(tt % 2) * 128 + 128],
                                ctm[:, :], id_f[:, :]).then_inc(tC3c, 1)

            @blk.vector
            def _(v):
                fc = [0]

                def vf(ins):
                    fc[0] += 1
                    ins.then_inc(vC3f, 1)
                    v.wait_ge(vC3f, fc[0])

                v.wait_ge(tC3, 32)
                vf(v.tensor_scalar(nrow[:1, :], pst[:1, :], -1.0 / D, None, ALU.mult))
                vf(v.tensor_scalar(vrow[:1, :], psq[:1, :], 1.0 / D, 1e-5, ALU.mult, ALU.add))
                vf(v.tensor_mul(tmprow[:1, :], nrow[:1, :], nrow[:1, :]))
                v.tensor_sub(vrow[:1, :], vrow[:1, :], tmprow[:1, :]).then_inc(vC3, 1)
                v.wait_ge(tC3, 33)
                v.tensor_copy(negm_b[:, :], pbc[:, :]).then_inc(vC3, 1)
                v.wait_ge(tC3, 34)
                v.tensor_copy(rstd_b[:, :], pbc[:, :]).then_inc(vC3, 1)
                for oc in range(HC):
                    v.tensor_add(x_fme[:, oc * TQ:(oc + 1) * TQ],
                                 x_fme[:, oc * TQ:(oc + 1) * TQ], negm_b[:, :])
                    v.tensor_mul(x_fme[:, oc * TQ:(oc + 1) * TQ],
                                 x_fme[:, oc * TQ:(oc + 1) * TQ], rstd_b[:, :])
                    v.tensor_copy(x_bf[:, oc * TQ:(oc + 1) * TQ],
                                  x_fme[:, oc * TQ:(oc + 1) * TQ]).then_inc(vC3, 1)
                for tt in range(OT if c3gate else 0):
                    v.wait_ge(tC3g, tt + 1)
                    vf(v.reduce_max(m16[:, :], pga[:, (tt % 2) * 32:(tt % 2) * 32 + 16],
                                    axis=AX.X, negate=True))
                    v.tensor_copy(m16[:, :], m16[:, :]).then_inc(vC3g, 1)
                    v.wait_ge(cC3, 34 + 2 * tt)
                    vf(v.reduce_sum(sfull[:, :], e16[:, :], axis=AX.X))
                    v.wait_ge(cC3, 35 + 2 * tt)
                    vf(v.tensor_scalar(p16[:, :], e16[:, :], sfull[:, :], None, ALU.mult))
                    vf(v.reduce_max(m16[:, :], p16[:, :], axis=AX.X, negate=True))
                    vf(v.tensor_scalar(msk16[:, :], p16[:, :], m16[:, :], None, ALU.add))
                    vf(v.tensor_scalar(msk16[:, :], msk16[:, :], 0.0, 1.0, ALU.is_lt, ALU.mult))
                    vf(v.tensor_mul(msk16[:, :], p16[:, :], msk16[:, :]))
                    v.tensor_copy(msk16[:, :], msk16[:, :]).then_inc(vC3g, 1)
                    vf(v.reduce_max(m16[:, :], msk16[:, :], axis=AX.X, negate=True))
                    vf(v.tensor_scalar(msk16[:, :], p16[:, :], m16[:, :], None, ALU.add))
                    vf(v.tensor_scalar(msk16[:, :], msk16[:, :], 0.0, 1.0, ALU.is_ge, ALU.mult))
                    vf(v.tensor_mul(ctm[:, :], p16[:, :], msk16[:, :]))
                    v.tensor_copy(ctm[:, 0:1], ctm[:, 0:1]).then_inc(vC3g, 1)
                    v.wait_ge(tC3c, tt + 1)
                    v.tensor_copy(comb[:, tt * 128:(tt + 1) * 128],
                                  pct[:, (tt % 2) * 128:(tt % 2) * 128 + 128]).then_inc(vC3g, 1)

        c3s.close()

        if debug:
            sDBG3 = sem("sDBG3")
            with nc.Block() as blk:
                @blk.sync
                def _(sy):
                    sy.dma_start(dbg["d_x1"][:, :], x_fme[:, :]).then_inc(sDBG3, 16)
                    sy.dma_start(dbg["d_comb"][:, :], comb[:, :]).then_inc(sDBG3, 16)
                    sy.dma_start(dbg["d_qfm"][:, 0:512], rstd_b[:, 0:512]).then_inc(sDBG3, 16)
                    sy.wait_ge(sDBG3, 48)

        if phases < 6:
            _final_barrier(nc, al)
            return nc
        c4s = al(ExitStack())
        pcy0 = psum("pcy0", [128, 128], F32, c4s)
        pcyr = [psum(f"pcyr{i}", [128, 128], F32, c4s) for i in range(2)]
        tC4 = sem("tC4")
        vC4 = sem("vC4")

        def sigma(d):
            tt, oc = d // 16, d % 16
            return oc * 8 + tt

        cycles = []
        seen = [False] * 128
        for d0 in range(128):
            if seen[d0]:
                continue
            cyc = [d0]
            seen[d0] = True
            d = sigma(d0)
            while d != d0:
                cyc.append(d)
                seen[d] = True
                d = sigma(d)
            cycles.append(cyc)

        t_plan = []
        v_plan = []
        gt = gv = 0
        wk_global = 0
        last_ev_of_slot = {0: None, 1: None}
        for cyc in cycles:
            L = len(cyc)
            t_plan.append(("c0", cyc[0], gv))
            gt += 1
            for ii in range(1, L):
                slot = wk_global % 2
                w_ev = last_ev_of_slot[slot]
                t_plan.append((slot, cyc[ii], (w_ev + 1) if w_ev is not None else 0))
                gt += 1
                v_plan.append((cyc[ii - 1], slot, gt))
                last_ev_of_slot[slot] = gv
                gv += 1
                wk_global += 1
            v_plan.append((cyc[L - 1], "c0", gt))
            gv += 1

        with nc.Block() as blk:
            @blk.tensor
            def _(t):
                for (slot, srcb, w_ev) in t_plan:
                    if w_ev > 0:
                        t.wait_ge(vC4, w_ev)
                    dstp = pcy0 if slot == "c0" else pcyr[slot]
                    t.transpose(dstp[:, :], x_fme[:, srcb * 128:(srcb + 1) * 128],
                                id_f[:, :]).then_inc(tC4, 1)

            @blk.vector
            def _(v):
                for (dstb, slot, w_tp) in v_plan:
                    v.wait_ge(tC4, w_tp)
                    srcp = pcy0 if slot == "c0" else pcyr[slot]
                    v.tensor_copy(x_fme[:, dstb * 128:(dstb + 1) * 128],
                                  srcp[:, :]).then_inc(vC4, 1)

        c4s.close()

        if phases < 7:
            _final_barrier(nc, al)
            return nc
        dsc = al(ExitStack())
        wd_sb = sb("wd_sb", [128, NCI * D], BF16, dsc)
        h_sb = sb("h_sb", [128, NCI * TQ], BF16, dsc)

        pg_ = [psum(f"pg{i}", [128, 512], F32, dsc) for i in range(2)]
        pu_ = [psum(f"pu{i}", [128, 512], F32, dsc) for i in range(2)]
        pdall = psum("pdall", [128, 2048], F32, dsc)
        sD_wd = sem("sD_wd")
        sD_cr = sem("sD_cr")
        tD_g = sem("tD_g")
        tD_u = sem("tD_u")
        cD_sl = sem("cD_sl")
        vD_u2 = sem("vD_u2")
        gD_h = sem("gD_h")
        tD_cb = sem("tD_cb")
        vD_cb = sem("vD_cb")
        tD_dm = sem("tD_dm")
        vD_pd = sem("vD_pd")

        with nc.Block() as blk:
            @blk.gpsimd
            def _(g):
                for ee in range(NE):
                    if ee > 0:
                        g.wait_ge(tD_dm, 32 * ee)
                    for part in range(2):
                        ic0 = part * 6
                        icn = 6 if part == 0 else 5
                        g.dma_start(
                            wd_sb[:, ic0 * D:(ic0 + icn) * D].rearrange(
                                "p (ic o) -> p ic o", ic=icn),
                            wdna[ee, ic0 * 128:(ic0 + icn) * 128, :].rearrange(
                                "(ic p) o -> p ic o", p=128)).then_inc(sD_wd, 16)
                    for c in range(NCI):
                        gc = ee * NCI + c
                        if gc + 3 < NE * NCI:
                            ee2, c2 = (gc + 3) // NCI, (gc + 3) % NCI
                            g.wait_ge(tD_u, 2 * (gc + 1))
                            g.dma_start(
                                wgu_sb[gc % 3][:, :].rearrange("p (hc j) -> p hc j", hc=HC),
                                wgua[ee2, :, c2 * 256:(c2 + 1) * 256].rearrange(
                                    "(hc p) j -> p hc j", p=128)).then_inc(sD_gu, 16)


            @blk.scalar
            def _(s):
                s.dma_start(crow[:1, :], comb[0:1, :]).then_inc(sD_cr, 16)
                for ee in range(NE):
                    if ee + 1 < E:
                        s.wait_ge(tD_cb, ee + 1)
                        s.dma_start(crow[:1, :], comb[ee + 1:ee + 2, :]).then_inc(sD_cr, 16)
                    for c in range(NCI):
                        for hf in range(2):
                            idx = 2 * (ee * NCI + c) + hf
                            s.wait_ge(tD_g, idx + 1)
                            if idx >= 3:
                                s.wait_ge(gD_h, idx - 2)
                            s.activation(sg_tmp[idx % 3][:, :], pg_[idx % 2][:, :],
                                         AF.Silu, bias=0.0, scale=1.0).then_inc(cD_sl, 1)

            @blk.tensor
            def _(t):
                t.wait_ge(sD_cr, 16)
                t.matmul(pg_[0][:, 0:512], ones_row[:1, :],
                         crow[:1, 0:512], start=True, stop=True)
                t.matmul(pg_[1][:, 0:512], ones_row[:1, :],
                         crow[:1, 512:1024], start=True,
                         stop=True).then_inc(tD_cb, 1)
                for ee in range(NE):
                    for c in range(NCI):
                        gc = ee * NCI + c
                        t.wait_ge(sD_gu, 16 * (gc + 1))
                        if c == 0 and ee < E:
                            t.wait_ge(vD_cb, ee + 1)
                        if gc >= 1:
                            t.wait_ge(cD_sl, 2 * gc)
                        last = [None, None]
                        for hf in range(2):
                            for hcc in range(HC):
                                last[hf] = t.matmul(
                                    pg_[hf][:, :],
                                    wgu_sb[gc % 3][:, hcc * 256:hcc * 256 + 128],
                                    x_bf[:, hcc * TQ + hf * 512:hcc * TQ + hf * 512 + 512],
                                    start=(hcc == 0), stop=(hcc == HC - 1))
                            last[hf].then_inc(tD_g, 1)
                        if gc >= 1:
                            t.wait_ge(vD_u2, 2 * gc)
                        for hf in range(2):
                            for hcc in range(HC):
                                last[hf] = t.matmul(
                                    pu_[hf][:, :],
                                    wgu_sb[gc % 3][:, hcc * 256 + 128:hcc * 256 + 256],
                                    x_bf[:, hcc * TQ + hf * 512:hcc * TQ + hf * 512 + 512],
                                    start=(hcc == 0), stop=(hcc == HC - 1))
                            last[hf].then_inc(tD_u, 1)
                    if ee + 1 < E:
                        t.wait_ge(cD_sl, 22 * (ee + 1))
                        t.wait_ge(sD_cr, 16 * (ee + 2))
                        t.matmul(pg_[0][:, 0:512], ones_row[:1, :],
                                 crow[:1, 0:512], start=True, stop=True)
                        t.matmul(pg_[1][:, 0:512], ones_row[:1, :],
                                 crow[:1, 512:1024], start=True,
                                 stop=True).then_inc(tD_cb, 1)
                    t.wait_ge(gD_h, 22 * (ee + 1))
                    t.wait_ge(sD_wd, 32 * (ee + 1))
                    for tt in range(OT):
                        for og in range(4):
                            gidx = (8 * ee + tt) * 4 + og
                            if gidx >= 4:
                                t.wait_ge(vD_pd, gidx - 3)
                            ins = None
                            for ic in range(NCI):
                                lhs = h_sb[:, ic * TQ + tt * 128:ic * TQ + (tt + 1) * 128]
                                ins = t.matmul(
                                    pdall[:, og * 512:(og + 1) * 512], lhs,
                                    wd_sb[:, ic * D + og * 512:ic * D + og * 512 + 512],
                                    start=(ic == 0), stop=(ic == NCI - 1))
                            ins.then_inc(tD_dm, 1)

            @blk.vector
            def _(v):
                for ee in range(NE):
                    if ee < E:
                        v.wait_ge(tD_cb, ee + 1)
                        v.tensor_copy(cb_sb[:, 0:512], pg_[0][:, 0:512])
                        v.tensor_copy(cb_sb[:, 512:1024], pg_[1][:, 0:512]).then_inc(vD_cb, 1)
                    if ee == E:
                        v.memset(cb_sb[:, :], 1.0)
                    for c in range(NCI):
                        for hf in range(2):
                            idx = 2 * (ee * NCI + c) + hf
                            v.wait_ge(tD_u, idx + 1)
                            v.tensor_mul(u2_tmp[idx % 3][:, :], pu_[idx % 2][:, :],
                                         cb_sb[:, hf * 512:hf * 512 + 512]).then_inc(vD_u2, 1)
                            v.wait_ge(cD_sl, idx + 1)
                            if ee > 0:
                                v.wait_ge(tD_dm, 32 * ee)
                            v.tensor_mul(h_sb[:, c * TQ + hf * 512:c * TQ + hf * 512 + 512],
                                         sg_tmp[idx % 3][:, :],
                                         u2_tmp[idx % 3][:, :]).then_inc(gD_h, 1)
                    for tt in range(OT):
                        for og in range(4):
                            gidx = (8 * ee + tt) * 4 + og
                            v.wait_ge(tD_dm, gidx + 1)
                            v.tensor_add(
                                x_fme[:, tt * D + og * 512:tt * D + og * 512 + 512],
                                x_fme[:, tt * D + og * 512:tt * D + og * 512 + 512],
                                pdall[:, og * 512:(og + 1) * 512]).then_inc(vD_pd, 1)

        dsc.close()

        if phases < 8:
            _final_barrier(nc, al)
            return nc
        esc = al(ExitStack())
        ysq = [sb(f"ysq{r}", [128, D], F32, esc) for r in range(2)]
        srow1 = [sb(f"srow1{r}", [128, 1], F32, esc) for r in range(2)]
        srow2 = [sb(f"srow2{r}", [128, 1], F32, esc) for r in range(2)]
        srow3 = sb("srow3", [128, 1], F32, esc)
        nmr = [sb(f"nmr{r}", [128, 1], F32, esc) for r in range(2)]
        out_sb = [sb(f"out_sb{r}", [128, D], F32, esc) for r in range(2)]
        cSq = sem("cSq")
        vR = sem("vR")
        vV = sem("vV")
        cL = sem("cL")
        vN = sem("vN")
        cE = sem("cE")
        sE = sem("sE")
        vEf = sem("vEf")

        with nc.Block() as blk:
            @blk.scalar
            def _(s):
                s.square(ysq[0][:, :], x_fme[:, 0:D]).then_inc(cSq, 1)
                for tt in range(OT):
                    if tt + 1 < OT:
                        if tt >= 1:
                            s.wait_ge(vR, tt)
                        s.square(ysq[(tt + 1) % 2][:, :],
                                 x_fme[:, (tt + 1) * D:(tt + 2) * D]).then_inc(cSq, 1)
                    s.wait_ge(vV, tt + 1)
                    s.activation(srow2[tt % 2][:, :], srow2[tt % 2][:, :],
                                 AF.Ln, bias=0.0, scale=1.0)
                    s.activation(srow2[tt % 2][:, :], srow2[tt % 2][:, :],
                                 AF.Exp, bias=0.0, scale=-0.5).then_inc(cL, 1)

            @blk.vector
            def _(v):
                fe = [0]

                def ve_f(ins):
                    fe[0] += 1
                    ins.then_inc(vEf, 1)
                    v.wait_ge(vEf, fe[0])

                for tt in range(OT):
                    ve_f(v.reduce_sum(srow1[tt % 2][:, :],
                                      x_fme[:, tt * D:(tt + 1) * D], axis=AX.X))
                    v.wait_ge(cSq, tt + 1)
                    if tt >= 2:
                        v.wait_ge(cE, tt - 1)
                    v.reduce_sum(srow2[tt % 2][:, :], ysq[tt % 2][:, :],
                                 axis=AX.X).then_inc(vR, 1)
                    ve_f(v.tensor_scalar(srow1[tt % 2][:, :], srow1[tt % 2][:, :],
                                         -1.0 / D, None, ALU.mult))
                    ve_f(v.tensor_scalar(srow2[tt % 2][:, :], srow2[tt % 2][:, :],
                                         1.0 / D, 1e-5, ALU.mult, ALU.add))
                    ve_f(v.tensor_mul(srow3[:, :], srow1[tt % 2][:, :], srow1[tt % 2][:, :]))
                    ve_f(v.tensor_sub(srow2[tt % 2][:, :], srow2[tt % 2][:, :], srow3[:, :]))
                    v.tensor_copy(srow2[tt % 2][:, :], srow2[tt % 2][:, :]).then_inc(vV, 1)
                    v.wait_ge(cL, tt + 1)
                    if tt >= 2:
                        v.wait_ge(sE, 16 * (tt - 1))
                    v.tensor_scalar(out_sb[tt % 2][:, :], x_fme[:, tt * D:(tt + 1) * D],
                                    srow1[tt % 2][:, :], srow2[tt % 2][:, :],
                                    ALU.add, ALU.mult).then_inc(cE, 1)

            @blk.sync
            def _(sy):
                for tt in range(OT):
                    sy.wait_ge(cE, tt + 1)
                    sy.dma_start(ya[tt * 128:(tt + 1) * 128, :],
                                 out_sb[tt % 2][:, :]).then_inc(sE, 16)
                sy.wait_ge(sE, 16 * OT)

        if debug:
            sDBG4 = sem("sDBG4")
            with nc.Block() as blk:
                @blk.sync
                def _(sy):
                    sy.dma_start(dbg["d_y"][:, :], x_fme[:, :]).then_inc(sDBG4, 16)
                    sy.wait_ge(sDBG4, 16)

        esc.close()

    return nc



def _host_inputs(inputs):
    x = np.ascontiguousarray(np.asarray(inputs["L0_attn_hidden_states"], dtype=np.float32))
    Wq = np.ascontiguousarray(np.asarray(inputs["L0_attn_Wq"], dtype=np.float32))
    Wk = np.ascontiguousarray(np.asarray(inputs["L0_attn_Wk"], dtype=np.float32))
    Wv = np.ascontiguousarray(np.asarray(inputs["L0_attn_Wv"], dtype=np.float32))
    Wo = np.ascontiguousarray(np.asarray(inputs["L0_attn_Wo"], dtype=np.float32))
    ve0 = np.ascontiguousarray(np.asarray(inputs["L0_attn_v_emb0"], dtype=np.float32)).reshape(128, 1)
    ve1 = np.ascontiguousarray(np.asarray(inputs["L0_attn_v_emb1"], dtype=np.float32)).reshape(128, 1)
    tau = np.ascontiguousarray(np.asarray(inputs["L0_attn_tau"], dtype=np.float32)).reshape(1, 1)
    gwm = np.ascontiguousarray(np.asarray(inputs["L0_ffn_gate_w"], dtype=np.float32))
    wgu0 = np.asarray(inputs["L0_ffn_w_gate_up"], dtype=np.float32)
    wdn0 = np.asarray(inputs["L0_ffn_w_down"], dtype=np.float32)
    sgu = np.asarray(inputs["L0_ffn_shared_gate_up"], dtype=np.float32)
    sdn = np.asarray(inputs["L0_ffn_shared_down"], dtype=np.float32)

    wgu_all = np.empty((NE, D, 2816), dtype=np.float32)
    g_ = wgu0[:, :, :I].reshape(E, D, NCI, 128)
    u_ = wgu0[:, :, I:].reshape(E, D, NCI, 128)
    wgu_all[:E] = np.stack([g_, u_], axis=3).reshape(E, D, 2816)
    sg_ = sgu[:, :2816].reshape(D, 22, 128)
    su_ = sgu[:, 2816:].reshape(D, 22, 128)
    sh = np.stack([sg_, su_], axis=2).reshape(D, 22, 256)
    wgu_all[E] = sh[:, :NCI, :].reshape(D, 2816)
    wgu_all[E + 1] = sh[:, NCI:, :].reshape(D, 2816)
    wgu_all = np.ascontiguousarray(wgu_all)

    wdn_all = np.empty((NE, I, D), dtype=np.float32)
    wdn_all[:E] = wdn0
    wdn_all[E] = sdn[:I]
    wdn_all[E + 1] = sdn[I:]
    wdn_all = np.ascontiguousarray(wdn_all)

    in_maps = []
    for core in range(8):
        b, c = core // 4, core % 4
        qv = np.asarray([(4 * t + c) * 128 + r for t in range(OT) for r in range(128)],
                        dtype=np.float32).reshape(1, TQ)
        in_maps.append({
            "xseq": np.ascontiguousarray(x[b]),
            "wq": Wq, "wk": Wk, "wv": Wv, "wo": Wo, "gw": gwm,
            "ve0": ve0, "ve1": ve1, "tau": tau,
            "qidx": qv,
            "cq": np.asarray([[c * 128]], dtype=np.int32),
            "wgu": wgu_all, "wdn": wdn_all,
        })
    return in_maps


def kernel(**inputs):
    in_maps = _host_inputs(inputs)
    nc = build_program(debug=False)
    res = run_bass_kernel_spmd(nc, in_maps, core_ids=list(range(8)))
    out = np.empty((2, S, D), dtype=np.float32)
    for core in range(8):
        b, c = core // 4, core % 4
        yc = np.asarray(res.results[core]["y"], dtype=np.float32).reshape(OT, 128, D)
        for t in range(OT):
            a = 4 * t + c
            out[b, a * 128:(a + 1) * 128, :] = yc[t]
    return out
